# version 22
# baseline (speedup 1.0000x reference)
import numpy as np
import concourse.bass as bass
import concourse.mybir as mybir
import concourse.bass_utils as bass_utils
from contextlib import ExitStack

F32 = mybir.dt.float32
BF16 = mybir.dt.bfloat16
I32 = mybir.dt.int32
AF = mybir.ActivationFunctionType
ALU = mybir.AluOpType
AX = mybir.AxisListType

T = 4096
D = 1024
HID = 3072
NDC = 8
MAGIC = 12582912.0
EPS = 1e-5

ENGINES = ("sync", "scalar", "vector", "tensor", "gpsimd")

PHASE_GROUPS = [(1, 2, 3, 4)]



class S:
    __slots__ = ("h", "n", "name")

    def __init__(self, name):
        self.h = None
        self.n = 0
        self.name = name


class Plan:
    def __init__(self):
        self.ops = {e: [] for e in ENGINES}
        self.lw = {e: {} for e in ENGINES}
        self.pending = {e: [] for e in ENGINES}
        self.engsem = {}

    def barrier(self, waits):
        for e in ENGINES:
            self.pending[e].extend(waits)

    def emit(self, eng, fn, waits=(), inc=None, by=1, chain=True):
        raw = list(self.pending[eng]) + list(waits)
        self.pending[eng] = []
        s_own = self.engsem.get(eng)
        if s_own is not None:
            if inc is None:
                inc = s_own
            if chain and s_own.n > 0:
                raw.append((s_own, s_own.n))
        wl = []
        for s, v in raw:
            if v <= 0:
                continue
            k = id(s)
            if self.lw[eng].get(k, 0) < v:
                wl.append((s, v))
                self.lw[eng][k] = v
        self.ops[eng].append((fn, tuple(wl), inc, by))
        if inc is not None:
            inc.n += by
            return inc.n
        return 0

    def dma(self, eng, out, in_, waits=(), inc=None):
        return self.emit(
            eng, lambda e, o=out, i=in_: e.dma_start(out=o, in_=i), waits, inc, 16)

    def mm(self, out, lhsT, rhs, start, stop, waits=(), inc=None, tp=None):
        if tp is None:
            fn = lambda e, o=out, l=lhsT, r=rhs, s1=start, s2=stop: e.matmul(
                o, l, r, start=s1, stop=s2)
        else:
            fn = lambda e, o=out, l=lhsT, r=rhs, s1=start, s2=stop, t=tp: e.matmul(
                o, l, r, start=s1, stop=s2, tile_position=t)
        return self.emit("tensor", fn, waits, inc)

    def trans(self, out, in_, ident, waits=(), inc=None):
        return self.emit(
            "tensor",
            lambda e, o=out, i=in_, d=ident: e.transpose(o, i, d), waits, inc)

    def act(self, out, in_, func, waits=(), inc=None, bias=None, scale=None):
        kw = {}
        if bias is not None:
            kw["bias"] = bias
        if scale is not None:
            kw["scale"] = scale
        return self.emit(
            "scalar",
            lambda e, o=out, i=in_, f=func, k=kw: e.activation(
                out=o, in_=i, func=f, **k), waits, inc)

    def scopy(self, out, in_, waits=(), inc=None):
        return self.emit(
            "scalar", lambda e, o=out, i=in_: e.copy(out=o, in_=i), waits, inc)

    def ts(self, out, in0, s1, s2, op0, op1=None, waits=(), inc=None,
           chain=True):
        if op1 is None:
            fn = lambda e, o=out, i=in0, a=s1, p0=op0: e.tensor_scalar(
                out=o, in0=i, scalar1=a, scalar2=None, op0=p0)
        else:
            fn = lambda e, o=out, i=in0, a=s1, b=s2, p0=op0, p1=op1: e.tensor_scalar(
                out=o, in0=i, scalar1=a, scalar2=b, op0=p0, op1=p1)
        return self.emit("vector", fn, waits, inc, chain=chain)

    def stt(self, out, in0, scalar, in1, op0, op1, waits=(), inc=None):
        return self.emit(
            "vector",
            lambda e, o=out, i=in0, s=scalar, j=in1, p0=op0, p1=op1:
            e.scalar_tensor_tensor(
                out=o, in0=i, scalar=s, in1=j, op0=p0, op1=p1), waits, inc)

    def tt(self, out, in0, in1, op, waits=(), inc=None, chain=True):
        return self.emit(
            "vector",
            lambda e, o=out, a=in0, b=in1, p=op: e.tensor_tensor(
                out=o, in0=a, in1=b, op=p), waits, inc, chain=chain)

    def vcopy(self, out, in_, waits=(), inc=None, chain=True):
        return self.emit(
            "vector", lambda e, o=out, i=in_: e.tensor_copy(out=o, in_=i),
            waits, inc, chain=chain)

    def vmax(self, out, in_, absval, waits=(), inc=None):
        return self.emit(
            "vector",
            lambda e, o=out, i=in_, a=absval: e.reduce_max(
                out=o, in_=i, axis=AX.X, apply_absolute_value=a), waits, inc)

    def vrecip(self, out, in_, waits=(), inc=None):
        return self.emit(
            "vector", lambda e, o=out, i=in_: e.reciprocal(out=o, in_=i),
            waits, inc)

    def memset(self, ap, v, waits=(), inc=None, eng="vector"):
        return self.emit(
            eng, lambda e, a=ap, x=v: e.memset(a, x), waits, inc)

    def iota(self, out, pattern, chm, waits=(), inc=None):
        return self.emit(
            "gpsimd",
            lambda e, o=out, p=pattern, c=chm: e.iota(
                o, pattern=p, base=0, channel_multiplier=c,
                allow_small_or_imprecise_dtypes=True), waits, inc)

    def wait_only(self, eng, waits):
        self.ops[eng].append((None, tuple(waits), None, 1))

    def realize(self, block):
        for eng in ENGINES:
            ops = self.ops[eng]
            if not ops:
                continue

            def section(e, _ops=ops):
                for fn, wl, inc, by in _ops:
                    for s, v in wl:
                        e.wait_ge(s.h, v)
                    if fn is None:
                        continue
                    ins = fn(e)
                    if inc is not None:
                        ins.then_inc(inc.h, by)

            getattr(block, eng)(section)


class Builder:

    def __init__(self, phases):
        self.phases = tuple(phases)
        self.nc = bass.Bass("TRN2", target_bir_lowering=False)
        self.plan = Plan()
        self.es = ExitStack()
        self.sems = {}
        for name in ("c0", "lxA", "lxB", "lxC", "lxD", "lwA", "lwB", "lwC",
                     "lwD", "gd", "gdB", "sc", "ve", "pe", "gp"):
            s = S(name)
            self.sems[name] = s
            setattr(self, name, s)
        self.sq_war = 0
        self.sq_sc_war = 0

    def declare_dram(self):
        nc = self.nc
        d = {}
        d["XIN"] = nc.dram_tensor("XIN", [D, T], F32, kind="ExternalInput").ap()
        d["XOUT"] = nc.dram_tensor("XOUT", [D, T], F32, kind="ExternalOutput").ap()
        ph = self.phases
        for i in range(len(ph) - 1):
            name = f"SCR{ph[i]}"
            d[name] = nc.dram_tensor(name, [D, T], F32, kind="Internal").ap()
        d["bvq"] = nc.dram_tensor("bvq", [128, 8], F32, kind="ExternalInput").ap()
        d["bvk"] = nc.dram_tensor("bvk", [128, 8], F32, kind="ExternalInput").ap()
        d["bvv"] = nc.dram_tensor("bvv", [128, 8], F32, kind="ExternalInput").ap()
        d["w0g"] = nc.dram_tensor("w0g", [D, HID], I32, kind="ExternalInput").ap()
        d["w0u"] = nc.dram_tensor("w0u", [D, HID], I32, kind="ExternalInput").ap()
        d["w0d"] = nc.dram_tensor("w0d", [HID, D], I32, kind="ExternalInput").ap()
        d["wqkv"] = nc.dram_tensor("wqkv", [D, 3 * D], F32, kind="ExternalInput").ap()
        d["bqkv"] = nc.dram_tensor("bqkv", [128, 24], F32, kind="ExternalInput").ap()
        d["lnw"] = nc.dram_tensor("lnw", [128, 8], F32, kind="ExternalInput").ap()
        d["lnb"] = nc.dram_tensor("lnb", [128, 8], F32, kind="ExternalInput").ap()
        d["pkd"] = nc.dram_tensor("pkd", [D, 256], I32, kind="ExternalInput").ap()
        d["wsc"] = nc.dram_tensor("wsc", [1, 1], F32, kind="ExternalInput").ap()
        d["bias1"] = nc.dram_tensor("bias1", [128, 8], F32, kind="ExternalInput").ap()
        self.d = d

    def act_io(self, phase):
        ph = self.phases
        i = ph.index(phase)
        src = self.d["XIN"] if i == 0 else self.d[f"SCR{ph[i - 1]}"]
        dst = self.d["XOUT"] if i == len(ph) - 1 else self.d[f"SCR{phase}"]
        return src, dst

    def alloc_persistent(self):
        nc, es = self.nc, self.es
        sb = lambda name, shape, dt: es.enter_context(nc.sbuf_tensor(name, shape, dt))
        self.ones_col = sb("ones_col", [128, 1], F32)
        self.ones_b16 = sb("ones_b16", [128, 1], BF16)
        self.ones_row = sb("ones_row", [1, 128], F32)
        self.iot128 = sb("iot128", [128, 128], F32)
        self.m128ge = sb("m128ge", [128, 128], BF16)
        self.m128lt = sb("m128lt", [128, 128], BF16)
        self.ident_f32 = sb("ident_f32", [128, 128], F32)
        self.ident_bf = sb("ident_bf", [128, 128], BF16)
        self.sgq = sb("sgq", [128, 8], F32)
        self.sgk = sb("sgk", [128, 8], F32)
        self.sgv = sb("sgv", [128, 8], F32)
        self.bq_all = sb("bq_all", [128, 24], F32)
        self.bq_s = sb("bq_s", [128, 8], F32)
        self.lnw_sb = sb("lnw_sb", [128, 8], F32)
        self.lnb_sb = sb("lnb_sb", [128, 8], F32)
        self.bias1_sb = sb("bias1_sb", [128, 8], F32)
        self.wsc_sb = sb("wsc_sb", [1, 1], F32)
        self.ws_col = sb("ws_col", [128, 1], F32)
        self.r_mean = sb("r_mean", [1, 512], F32)
        self.r_var = sb("r_var", [1, 512], F32)
        self.sq = [sb(f"sqb{i}", [128, 512], F32) for i in range(2)]
        self.sq_ve_war = 0
        ps = lambda name, shape, dt: es.enter_context(nc.psum_tensor(name, shape, dt))
        self.pW0 = ps("pW0", [128, 1536], F32)
        self.pW1 = ps("pW1", [128, 1536], F32)
        self.p6 = ps("p6", [128, 512], F32)
        self.p7 = ps("p7", [128, 512], F32)
        for s in self.sems.values():
            s.h = es.enter_context(nc.semaphore(s.name))
        self.plan.engsem = {"vector": self.ve, "scalar": self.sc,
                            "gpsimd": self.gp}

    def emit_setup(self):
        p, d = self.plan, self.d
        ld, sc, ve, pe, gp = self.c0, self.sc, self.ve, self.pe, self.gp
        p.memset(self.ones_col[:], 1.0, inc=ve)
        p.memset(self.ones_b16[:], 1.0, inc=ve)
        p.memset(self.ones_row[:], 1.0, inc=ve)
        ones_ve = ve.n
        p.iota(self.iot128[:], [[1, 128]], -1, inc=gp)
        g2 = gp.n
        p.ts(self.m128ge[:], self.iot128[:], 0.0, None, ALU.is_ge,
             waits=[(gp, g2)], inc=ve)
        p.ts(self.m128lt[:], self.iot128[:], -1.0, None, ALU.is_le, inc=ve)
        p.ts(self.ident_f32[:], self.iot128[:], 0.0, None, ALU.is_equal, inc=ve)
        p.ts(self.ident_bf[:], self.iot128[:], 0.0, None, ALU.is_equal, inc=ve)
        p.dma("sync", self.sgq[:], d["bvq"][:], inc=ld)
        p.dma("sync", self.sgk[:], d["bvk"][:], inc=ld)
        p.dma("sync", self.sgv[:], d["bvv"][:], inc=ld)
        p.dma("sync", self.bq_all[:], d["bqkv"][:], inc=ld)
        p.dma("sync", self.lnw_sb[:], d["lnw"][:], inc=ld)
        p.dma("sync", self.lnb_sb[:], d["lnb"][:], inc=ld)
        p.dma("sync", self.bias1_sb[:], d["bias1"][:], inc=ld)
        p.dma("sync", self.wsc_sb[:], d["wsc"][:], inc=ld)
        n_ld = ld.n
        p.act(self.sgq[:], self.sgq[:], AF.Sign, waits=[(ld, n_ld)], inc=sc)
        p.act(self.sgk[:], self.sgk[:], AF.Sign, inc=sc)
        p.act(self.sgv[:], self.sgv[:], AF.Sign, inc=sc)
        n_sc = sc.n
        p.tt(self.sgq[:], self.sgq[:], self.sgk[:], ALU.mult,
             waits=[(sc, n_sc)], inc=ve)
        p.ts(self.bq_s[:], self.bq_all[:, 0:8], 1.0 / 512.0, None, ALU.mult,
             waits=[(ld, n_ld)], inc=ve)
        p.mm(self.p6[:, 0:1], self.ones_row[:], self.wsc_sb[:],
             True, True, waits=[(ld, n_ld), (ve, ones_ve)], inc=pe)
        p.vcopy(self.ws_col[:], self.p6[:, 0:1], waits=[(pe, pe.n)], inc=ve)
        self.setup_ve = ve.n
        self.p6_setup_war = ve.n

    def emit_stats(self, get_blk, width, n_blk, ab, bb, wait_x, war_bank=(),
                   cast_bufs=None):
        p = self.plan
        sc, ve, pe = self.sc, self.ve, self.pe
        sum_row = ab[0:1, 0:width]
        ss_row = bb[0:1, 0:width]
        n = 128 * n_blk
        ss_cnt = {}
        for i in range(n_blk):
            blk = get_blk(i)
            w = list(wait_x)
            if i - 2 in ss_cnt:
                w.append((pe, ss_cnt[i - 2]))
            elif self.sq_war:
                w.append((pe, self.sq_war))
            if i < 2 and self.sq_ve_war:
                w.append((ve, self.sq_ve_war))
            if cast_bufs is None:
                sqb = self.sq[i % 2][:, 0:width]
                p.act(sqb, blk, AF.Square, waits=w, inc=sc)
                sq_done = sc.n
                p.mm(sum_row, self.ones_col[:], blk, i == 0, i == n_blk - 1,
                     waits=list(wait_x) + (list(war_bank) if i == 0 else []),
                     inc=pe)
                p.mm(ss_row, self.ones_col[:], sqb, i == 0, i == n_blk - 1,
                     waits=[(sc, sq_done)], inc=pe)
            else:
                xb = cast_bufs[0][i % 2][:, 0:width]
                sqb = cast_bufs[1][i % 2][:, 0:width]
                p.vcopy(xb, blk, waits=w, inc=ve)
                xb_done = ve.n
                p.act(sqb, blk, AF.Square, waits=w, inc=sc)
                sq_done = sc.n
                p.mm(sum_row, self.ones_b16[:], xb, i == 0, i == n_blk - 1,
                     waits=[(ve, xb_done)] +
                     (list(war_bank) if i == 0 else []), inc=pe)
                p.mm(ss_row, self.ones_b16[:], sqb, i == 0, i == n_blk - 1,
                     waits=[(sc, sq_done)], inc=pe)
            ss_cnt[i] = pe.n
        self.sq_war = pe.n
        mm_done = pe.n
        rm = self.r_mean[:, 0:width]
        rv = self.r_var[:, 0:width]
        p.ts(rm, sum_row, 1.0 / n, None, ALU.mult, waits=[(pe, mm_done)], inc=ve)
        p.tt(rv, rm, rm, ALU.mult, inc=ve)
        p.stt(rv, ss_row, 1.0 / n, rv, ALU.mult, ALU.subtract, inc=ve)
        p.ts(rv, rv, EPS, None, ALU.add, inc=ve)
        rows_ve = ve.n
        p.act(rv, rv, AF.Ln, waits=[(ve, rows_ve)], inc=sc)
        p.act(rv, rv, AF.Exp, scale=-0.5, inc=sc)
        a_sc = sc.n
        p.stt(rm, rm, -1.0, rv, ALU.mult, ALU.mult,
              waits=[(sc, a_sc)], inc=ve)
        b_ve = ve.n
        p.mm(ab, self.ones_row[:], rv, True, True,
             waits=[(sc, a_sc)], inc=pe)
        p.mm(bb, self.ones_row[:], rm, True, True, waits=[(ve, b_ve)], inc=pe)
        return pe.n

    def emit_apply(self, get_xblk, get_out, width, n_blk, ab, bb, bc):
        p = self.plan
        ve, pe = self.ve, self.pe
        self.apply_dc = []
        for dc in range(n_blk):
            tm = self.sq[dc % 2][:, 0:width]
            w = [(pe, bc)]
            if dc < 2 and self.sq_war:
                w.append((pe, self.sq_war))
            p.tt(tm, get_xblk(dc), ab, ALU.mult, waits=w, inc=ve)
            p.tt(get_out(dc), tm, bb, ALU.add, inc=ve)
            self.apply_dc.append(ve.n)
        self.sq_ve_war = ve.n
        return ve.n

    def emit_p1(self):
        nc, p, d = self.nc, self.plan, self.d
        gd, sc, ve, pe = self.gd, self.sc, self.ve, self.pe
        lx = [self.lxA, self.lxB]
        src, dst = self.act_io(1)
        es1 = ExitStack()
        sb = lambda name, shape, dt: es1.enter_context(nc.sbuf_tensor(name, shape, dt))
        xn = sb("p1_xn", [128, NDC * T], BF16)
        xcol = [sb(f"p1_xcol{i}", [128, T], F32) for i in range(2)]
        xb16 = [sb(f"p1_xb16{i}", [128, 512], BF16) for i in range(2)]
        sqb16 = [sb(f"p1_sqb16{i}", [128, 512], BF16) for i in range(2)]
        iot512 = sb("p1_iot512", [128, 512], F32)
        m512 = sb("p1_m512", [128, 4 * 512], BF16)
        vtmp = [sb(f"p1_vtmp{i}", [128, 128], F32) for i in range(2)]
        src_r = src.rearrange("(c p) t -> p c t", p=128)
        p.iota(iot512[:], [[1, 512]], -1, inc=self.gp)
        g2 = self.gp.n
        for q in range(4):
            p.ts(m512[:, q * 512:(q + 1) * 512], iot512[:],
                 128.0 * q, None, ALU.is_ge, waits=[(self.gp, g2)], inc=ve)

        abL = [self.pW0[:, 0:512], self.pW1[:, 0:512]]
        bbL = [self.pW0[:, 512:1024], self.pW1[:, 512:1024]]
        apply_done = {}
        for tc in range(8):
            buf = xcol[tc % 2]
            lds = lx[tc % 2]
            ab = abL[tc % 2]
            bb = bbL[tc % 2]
            w = [(ve, apply_done[tc - 2])] if tc - 2 in apply_done else []
            br = buf.ap().rearrange("p (c j) -> p c j", c=NDC)
            p.dma("sync", br[:, 0:4], src_r[:, 0:4, tc * 512:(tc + 1) * 512],
                  waits=w, inc=lds)
            nld = lds.n
            ldg = [self.lwC, self.lwD][tc % 2]
            p.dma("gpsimd", br[:, 4:8], src_r[:, 4:8, tc * 512:(tc + 1) * 512],
                  waits=w, inc=ldg)
            nldg = ldg.n
            bc = self.emit_stats(
                lambda i, b=buf: b[:, i * 512:(i + 1) * 512], 512, NDC,
                ab, bb, [(lds, nld), (ldg, nldg)],
                war_bank=[(ve, apply_done[tc - 2])] if tc - 2 in apply_done
                else [], cast_bufs=(xb16, sqb16))
            self.emit_apply(
                lambda dc, b=buf: b[:, dc * 512:(dc + 1) * 512],
                lambda dc, t=tc: xn[:, dc * T + t * 512: dc * T + (t + 1) * 512],
                512, NDC, ab, bb, bc)
            apply_done[tc] = ve.n
        p1a_ve = ve.n

        xc = sb("p1_xc", [128, T], F32)
        x1c = sb("p1_x1c", [128, T], F32)
        kc = sb("p1_kc", [128, 32 * 128], BF16)
        vc = sb("p1_vc", [128, 32 * 128], BF16)
        at_e = [sb(f"p1_ate{i}", [128, 1536], BF16) for i in range(2)]
        at_o = [sb(f"p1_ato{i}", [128, 1536], BF16) for i in range(2)]
        O_e = self.p6
        O_o = self.p7
        TbV = [self.pW0[:, 1024:1152], self.pW1[:, 1024:1152]]
        TbVb = [self.pW0.bitcast(BF16)[:, 2048:2176],
                self.pW1.bitcast(BF16)[:, 2048:2176]]

        sig_e_cnt = [0]
        sig_o_cnt = [0]
        av_cnt = [0, 0]
        gg = 0
        evac_last_c = 0
        store_cnt = 0
        tb_evac = 0
        o_war_ve = self.p6_setup_war
        vtmp_war = 0

        for c in range(NDC):
            p.dma("sync", xc[:], src[c * 128:(c + 1) * 128, :],
                  waits=[(ve, evac_last_c)], inc=self.lxC)
            xc_ld = self.lxC.n
            x1c_war = store_cnt
            kbuild = {}
            vbuild = {}
            prev_c_pe = pe.n
            sig_war = []
            if sig_e_cnt[0]:
                sig_war.append((sc, sig_e_cnt[0]))
            if sig_o_cnt[0]:
                sig_war.append((sc, sig_o_cnt[0]))
            tev = {}
            for kt in range(32):
                p.ts(kc[:, kt * 128:(kt + 1) * 128],
                     xn[:, c * T + kt * 128: c * T + (kt + 1) * 128],
                     self.sgq[:, c:c + 1], None, ALU.mult,
                     waits=[(ve, p1a_ve), (pe, prev_c_pe)], inc=ve,
                     chain=(kt == 0))
                kbuild[kt] = ve.n
            for kt in range(32):
                p.ts(vc[:, kt * 128:(kt + 1) * 128],
                     xn[:, c * T + kt * 128: c * T + (kt + 1) * 128],
                     self.sgv[:, c:c + 1], None, ALU.mult,
                     waits=[(ve, p1a_ve)], inc=ve, chain=(kt == 0))
                vbuild[kt] = ve.n
            prem_ve = ve.n
            for kt in range(32):
                tw = [(ve, prem_ve)] + (sig_war if kt < 2 else [])
                if kt - 2 in tev:
                    tw.append((ve, vbuild[kt - 2]))
                else:
                    tw.append((ve, tb_evac))
                p.trans(TbVb[kt % 2], vc[:, kt * 128:(kt + 1) * 128],
                        self.ident_bf[:], waits=tw, inc=pe)
                tev[kt] = pe.n
                p.vcopy(vc[:, kt * 128:(kt + 1) * 128], TbVb[kt % 2],
                        waits=[(pe, tev[kt])], inc=ve, chain=False)
                vbuild[kt] = ve.n
            tb_evac = ve.n
            vtmp_war = pe.n
            for qs in range(8):
                nkt = 4 * qs + 4
                kts_all = list(range(nkt))[::-1]
                groups = [kts_all[i0:i0 + 3] for i0 in range(0, nkt, 3)]
                pending = None

                def emit_av(pend, nkt=nkt):
                    grp, sig_e, sig_o, maskv = pend
                    glen = len(grp)
                    for h, atp_pair, sig, Ob in ((0, at_e, sig_e, O_e),
                                                 (1, at_o, sig_o, O_o)):
                        base = 64 * h
                        atp = atp_pair[sig[0]]
                        for i, kt in enumerate(grp):
                            w = [(sc, sig[1]), (ve, vbuild[kt])]
                            if maskv:
                                w.append((ve, maskv))
                            if kt == nkt - 1:
                                w.append((ve, o_war_ve))
                            p.mm(Ob[base:base + 64, :],
                                 vc[:, kt * 128 + base: kt * 128 + base + 64],
                                 atp[:, i * 512:(i + 1) * 512],
                                 kt == nkt - 1, kt == 0,
                                 waits=w,
                                 inc=pe if i == glen - 1 else None,
                                 tp=(0, 64) if h == 1 else None)
                    av_cnt[sig_e[0]] = pe.n

                for gi, grp in enumerate(groups):
                    par = gg % 2
                    gg += 1
                    glen = len(grp)
                    for h, Sp, sig in ((0, self.pW0, sig_e_cnt),
                                       (1, self.pW1, sig_o_cnt)):
                        base = 64 * h
                        for i, kt in enumerate(grp):
                            p.mm(Sp[:, i * 512:(i + 1) * 512],
                                 kc[base:base + 64, kt * 128:(kt + 1) * 128],
                                 xn[base:base + 64,
                                    c * T + qs * 512: c * T + (qs + 1) * 512],
                                 True, True,
                                 waits=[(ve, kbuild[kt]), (sc, sig[0]),
                                        (ve, p1a_ve), (ve, tb_evac)],
                                 inc=pe)
                        sig.append(pe.n)
                    mm_e_done = sig_e_cnt.pop()
                    mm_o_done = sig_o_cnt.pop()
                    if pending is not None:
                        emit_av(pending)
                        pending = None
                    p.act(at_e[par][:, 0:glen * 512], self.pW0[:, 0:glen * 512],
                          AF.Sigmoid, scale=0.5,
                          waits=[(pe, mm_e_done), (pe, av_cnt[par])], inc=sc)
                    sig_e_cnt[0] = sc.n
                    p.act(at_o[par][:, 0:glen * 512], self.pW1[:, 0:glen * 512],
                          AF.Sigmoid, scale=0.5, waits=[(pe, mm_o_done)], inc=sc)
                    sig_o_cnt[0] = sc.n
                    mask_ve = 0
                    for i, kt in enumerate(grp):
                        pp = kt - 4 * qs
                        if pp >= 0:
                            for atp in (at_e[par], at_o[par]):
                                sl = atp[:, i * 512:(i + 1) * 512]
                                p.tt(sl, sl, m512[:, pp * 512:(pp + 1) * 512],
                                     ALU.mult, waits=[(sc, sig_o_cnt[0])], inc=ve)
                            mask_ve = ve.n
                    pending = (grp, (par, sig_e_cnt[0]), (par, sig_o_cnt[0]),
                               mask_ve)
                emit_av(pending)
                w = [(pe, pe.n), (self.lxC, xc_ld)]
                if x1c_war:
                    w.append((gd, x1c_war))
                p.tt(x1c[0:64, qs * 512:(qs + 1) * 512],
                     xc[0:64, qs * 512:(qs + 1) * 512],
                     O_e[0:64, :], ALU.add, waits=w, inc=ve)
                p.tt(x1c[64:128, qs * 512:(qs + 1) * 512],
                     xc[64:128, qs * 512:(qs + 1) * 512],
                     O_o[64:128, :], ALU.add, inc=ve)
                o_war_ve = ve.n
            evac_last_c = ve.n
            p.dma("gpsimd", dst[c * 128:(c + 1) * 128, :], x1c[:],
                  waits=[(ve, evac_last_c)], inc=gd)
            store_cnt = gd.n
        es1.close()
        return gd.n

    def emit_p2(self):
        nc, p, d = self.nc, self.plan, self.d
        gd, sc, ve, pe = self.gd, self.sc, self.ve, self.pe
        lw = [self.lwA, self.lwB]
        lx = self.lxA
        src, dst = self.act_io(2)
        W2 = 512
        NT2 = T // W2
        es2 = ExitStack()
        sb = lambda name, shape, dt: es2.enter_context(nc.sbuf_tensor(name, shape, dt))
        wg = sb("p2_wg", [128, NDC * HID], BF16)
        wu = sb("p2_wu", [128, NDC * HID], BF16)
        wd = sb("p2_wd", [128, 24 * D], BF16)
        x1col = sb("p2_x1col", [128, T], F32)
        src_r0 = src.rearrange("(c p) t -> p c t", p=128)
        p.dma("sync", x1col.ap().rearrange("p (c j) -> p c j", c=NDC),
              src_r0[:, :, 0:W2], inc=self.lxA)
        tc0_ld = self.lxA.n
        xres = [sb(f"p2_xres{i}", [128, 512], F32) for i in range(2)]
        esw = ExitStack()
        wload = [esw.enter_context(
            nc.sbuf_tensor(f"p2_wl{i}", [128, HID // 2], I32)) for i in range(2)]
        cvt_done = {}
        pend_scale = {}
        scale_ve = {}
        w_ready = {}
        gidx = 0
        for mat, (wsb, wt, nch, chw) in enumerate(
                [(wg, d["w0g"], NDC, HID), (wu, d["w0u"], NDC, HID),
                 (wd, d["w0d"], 24, D)]):
            for chunk in range(nch):
                for half in range(2):
                    hw2 = chw // 2
                    dst_ap = wsb[:, chunk * chw + half * hw2:
                                 chunk * chw + (half + 1) * hw2]
                    src_ap = wt[chunk * 128:(chunk + 1) * 128,
                                half * hw2:(half + 1) * hw2]
                    if gidx % 2 == 0:
                        buf = wload[0][:, 0:hw2]
                        lws = lw[0]
                        cw = cvt_done.get(gidx - 2)
                        p.dma("sync", buf, src_ap, waits=[cw] if cw else [],
                              inc=lws)
                        p.act(dst_ap, buf, AF.Copy, scale=1.0 / 127.0,
                              waits=[(lws, lws.n)], inc=sc)
                        cvt_done[gidx] = (sc, sc.n)
                    else:
                        cs = [self.lwC, self.lwD][(gidx // 2) % 2]
                        if pend_scale.get(id(cs)):
                            da, ds = pend_scale[id(cs)].pop(0)
                            p.ts(da, da, 1.0 / 127.0, None, ALU.mult,
                                 waits=[ds], inc=ve)
                            scale_ve[id(cs)] = ve.n
                        wcd = [(ve, scale_ve[id(cs)])] if scale_ve.get(id(cs)) \
                            else []
                        p.dma("gpsimd", dst_ap, src_ap, waits=wcd, inc=cs)
                        pend_scale.setdefault(id(cs), []).append(
                            (dst_ap, (cs, cs.n)))
                    gidx += 1
            for key in list(pend_scale):
                for da, ds in pend_scale[key]:
                    p.ts(da, da, 1.0 / 127.0, None, ALU.mult, waits=[ds],
                         inc=ve)
                    scale_ve[key] = ve.n
                pend_scale[key] = []
            w_ready[mat] = [(sc, sc.n), (ve, ve.n)]
        w_ready_sc = sc.n
        w_ready_ve = ve.n
        esw.close()
        xn1 = sb("p2_xn1", [128, T], BF16)
        hcol = sb("p2_h", [128, 24 * W2], BF16)

        ab = self.pW1[:, 512:1024]
        bb = self.pW1[:, 1024:1536]
        gb = [self.pW0[:, 0:512], self.pW0[:, 512:1024]]
        ub = [self.pW0[:, 1024:1536], self.pW1[:, 0:512]]
        Y = self.p6[:, 0:512]

        src_r = src.rearrange("(c p) t -> p c t", p=128)
        dst_w = dst.rearrange("(c p) t -> p c t", p=128)

        apply_of = {}
        store_done = {}
        sil_done = {}
        mul_done = {}
        evac_done = {}
        down_all_pe = 0

        def stage(tcn):
            lds = self.lxA
            prev_apply = apply_of.get(tcn - 1, 0)
            if tcn == 0:
                nld = tc0_ld
            else:
                p.dma("sync", x1col.ap().rearrange("p (c j) -> p c j", c=NDC),
                      src_r[:, :, tcn * W2:(tcn + 1) * W2],
                      waits=[(ve, prev_apply)], inc=lds)
                nld = lds.n
            bc = self.emit_stats(
                lambda i, b=x1col: b[:, i * W2:(i + 1) * W2], W2, NDC,
                ab, bb, [(lds, nld)],
                war_bank=[(ve, prev_apply)] if prev_apply else [])
            apply_of[tcn] = self.emit_apply(
                lambda dc, b=x1col: b[:, dc * W2:(dc + 1) * W2],
                lambda dc: xn1[:, dc * W2:(dc + 1) * W2],
                W2, NDC, ab, bb, bc)

        stage(0)
        for tc in range(NT2):
            apply_done = apply_of[tc]
            for ht in range(24):
                gbk = gb[ht % 2]
                ubk = ub[ht % 2]
                pht = ht - 2 if ht >= 2 else (22 + ht if tc > 0 else None)
                war_g = [(sc, sil_done[pht])] if pht in sil_done else []
                war_u = [(ve, mul_done[pht])] if pht in mul_done else []
                for dc in range(NDC):
                    p.mm(gbk, wg[:, dc * HID + ht * 128: dc * HID + (ht + 1) * 128],
                         xn1[:, dc * W2:(dc + 1) * W2], dc == 0, dc == NDC - 1,
                         waits=[(ve, apply_done)] + w_ready[0] +
                         (war_g if dc == 0 else []),
                         inc=pe if dc == NDC - 1 else None)
                g_done = pe.n
                for dc in range(NDC):
                    p.mm(ubk, wu[:, dc * HID + ht * 128: dc * HID + (ht + 1) * 128],
                         xn1[:, dc * W2:(dc + 1) * W2], dc == 0, dc == NDC - 1,
                         waits=w_ready[1] + (war_u if dc == 0 else []),
                         inc=pe if dc == NDC - 1 else None)
                u_done = pe.n
                hsl = hcol[:, ht * W2:(ht + 1) * W2]
                hw0 = [(pe, g_done)]
                if down_all_pe:
                    hw0.append((pe, down_all_pe))
                p.act(hsl, gbk, AF.Silu, waits=hw0, inc=sc)
                sil_done[ht] = sc.n
                p.tt(hsl, hsl, ubk, ALU.mult,
                     waits=[(sc, sil_done[ht]), (pe, u_done)], inc=ve)
                mul_done[ht] = ve.n
            h_done = ve.n
            if tc + 1 < NT2:
                stage(tc + 1)
            for dt in range(NDC):
                lr = [self.lxC, self.lxD][dt % 2]
                pdt2 = dt - 2 if dt - 2 >= 0 else dt + 6
                wl = [store_done[pdt2]] if pdt2 in store_done else []
                p.dma("sync", xres[dt % 2][:],
                      src[dt * 128:(dt + 1) * 128, tc * W2:(tc + 1) * W2],
                      waits=wl, inc=lr)
                lr_n = lr.n
                pdt = dt - 1 if dt >= 1 else (7 if tc > 0 else None)
                war_y = [(ve, evac_done[pdt])] if pdt in evac_done else []
                for hc in range(24):
                    p.mm(Y, wd[:, hc * D + dt * 128: hc * D + (dt + 1) * 128],
                         hcol[:, hc * W2:(hc + 1) * W2], hc == 0, hc == 23,
                         waits=[(ve, h_done)] + w_ready[2] +
                         (war_y if hc == 0 else []),
                         inc=pe if hc == 23 else None)
                y_done = pe.n
                gds = [gd, self.gdB][dt % 2]
                wst = [(pe, y_done), (lr, lr_n)]
                p.tt(xres[dt % 2][:], xres[dt % 2][:], Y, ALU.add,
                     waits=wst, inc=ve)
                evac_done[dt] = ve.n
                p.dma("gpsimd",
                      dst[dt * 128:(dt + 1) * 128, tc * W2:(tc + 1) * W2],
                      xres[dt % 2][:], waits=[(ve, ve.n)], inc=gds)
                store_done[dt] = (gds, gds.n)
            down_all_pe = pe.n
        es2.close()
        return gd.n

    def emit_p3(self):
        nc, p, d = self.nc, self.plan, self.d
        gd, sc, ve, pe = self.gd, self.sc, self.ve, self.pe
        lw = [self.lwA, self.lwB]
        lx = self.lxA
        src, dst = self.act_io(3)
        if 4 in self.phases:
            self._p4u_alloc()
        es3 = ExitStack()
        sb = lambda name, shape, dt: es3.enter_context(nc.sbuf_tensor(name, shape, dt))
        wq = sb("p3_wq", [128, NDC * 3 * D], BF16)
        wq_ch = {}
        for chunk in range(NDC):
            w = [(self.lwC, self.lwC.n)] if chunk > 0 else []
            p.dma("gpsimd", wq[:, chunk * 3 * D:(chunk + 1) * 3 * D],
                  d["wqkv"][chunk * 128:(chunk + 1) * 128, :], waits=w,
                  inc=self.lwC)
            wq_ch[chunk] = self.lwC.n
        wq_lwA = self.lwC.n

        x2col2 = [sb(f"p3_x2col{i}", [128, T], F32) for i in range(2)]
        xn2 = sb("p3_xn2", [128, T], BF16)
        qbuf = sb("p3_q", [128, NDC * 512], BF16)
        kbuf = sb("p3_k", [128, NDC * 1024], BF16)
        vbuf = sb("p3_v", [128, NDC * 1024], BF16)
        vtm = sb("p3_vtm", [128, 8 * NDC * 128], BF16)
        sT = sb("p3_sT", [128, 384], BF16)
        osb4 = sb("p3_osb4", [128, 4096], F32)
        abS = sb("p3_abS", [128, 512], F32)
        bbS = sb("p3_bbS", [128, 512], F32)
        xb16 = [sb(f"p3_xb16{i}", [128, 512], BF16) for i in range(2)]
        sqb16 = [sb(f"p3_sqb16{i}", [128, 512], BF16) for i in range(2)]

        qk_bank = [self.pW0[:, 0:512], self.pW0[:, 512:1024]]
        S3 = self.pW0[:, 1024:1408]
        Oa = self.pW1[:, 0:512]
        Ob = self.pW1[:, 512:1024]
        ab_t = self.p6
        bb_t = self.pW1[:, 1024:1536]
        Tb = self.p7[:, 0:128]
        Tbb = self.p7.bitcast(BF16)[:, 0:128]

        src_r = src.rearrange("(c p) t -> p c t", p=128)
        dst_w = dst.rearrange("(c p) t -> p c t", p=128)

        store_done = {}
        post_apply_ve = self.p6_setup_war
        pending_post = None
        evac_qk = {}
        s3_war_ve = 0
        s3_war_sc = 0
        self._p7war_ve = 0
        o_war = 0
        for tc in range(8):
            x2col = x2col2[tc % 2]
            lds = [self.lxA, self.lxB][tc % 2]
            w = [store_done[tc - 2]] if tc - 2 in store_done else []
            p.dma("sync", x2col.ap().rearrange("p (c j) -> p c j", c=NDC),
                  src_r[:, :, tc * 512:(tc + 1) * 512], waits=w, inc=lds)
            nld = lds.n
            bc = self.emit_stats(
                lambda i, b=x2col: b[:, i * 512:(i + 1) * 512], 512, NDC,
                ab_t[:], bb_t, [(lds, nld)],
                war_bank=[(ve, post_apply_ve)], cast_bufs=(xb16, sqb16))
            apply_ve = self.emit_apply(
                lambda dc, b=x2col: b[:, dc * 512:(dc + 1) * 512],
                lambda dc: xn2[:, dc * 512:(dc + 1) * 512],
                512, NDC, ab_t[:], bb_t, bc)
            if pending_post is not None:
                post_apply_ve = pending_post()
                pending_post = None
            s0 = (4 * tc) % 8
            for ot in range(24):
                bank = qk_bank[ot % 2]
                pot = ot - 2 if ot >= 2 else (22 + ot if tc > 0 else None)
                war = [(sc, evac_qk[pot])] if pot in evac_qk else []
                for dc in range(NDC):
                    p.mm(bank,
                         wq[:, dc * 3 * D + ot * 128: dc * 3 * D + (ot + 1) * 128],
                         xn2[:, dc * 512:(dc + 1) * 512], dc == 0, dc == NDC - 1,
                         waits=[(ve, apply_ve), (self.lwC, wq_ch[dc])] +
                         (war if dc == 0 else []),
                         inc=pe if dc == NDC - 1 else None)
                mmd = pe.n
                ew = [(pe, mmd)]
                if ot < 8:
                    out_ap = qbuf[:, ot * 512:(ot + 1) * 512]
                    p.act(out_ap, bank, AF.Identity, bias=self.bq_s[:, ot:ot + 1],
                          scale=1.0 / 512.0, waits=ew, inc=sc)
                elif ot < 16:
                    kk = ot - 8
                    out_ap = kbuf[:, kk * 1024 + s0 * 128: kk * 1024 + s0 * 128 + 512]
                    p.act(out_ap, bank, AF.Identity,
                          bias=self.bq_all[:, ot:ot + 1], waits=ew, inc=sc)
                else:
                    vv = ot - 16
                    out_ap = vbuf[:, vv * 1024 + s0 * 128: vv * 1024 + s0 * 128 + 512]
                    p.act(out_ap, bank, AF.Identity,
                          bias=self.bq_all[:, ot:ot + 1], waits=ew, inc=sc)
                evac_qk[ot] = sc.n
            qkv_sc = sc.n
            for i in range(4):
                kt = 4 * tc + i
                slot = kt % 8
                for dc in range(NDC):
                    p.trans(Tbb,
                            vbuf[:, dc * 1024 + slot * 128: dc * 1024 + (slot + 1) * 128],
                            self.ident_bf[:],
                            waits=[(sc, qkv_sc), (ve, self._p7war_ve)], inc=pe)
                    p.vcopy(vtm[:, (slot * NDC + dc) * 128:(slot * NDC + dc + 1) * 128],
                            Tbb, waits=[(pe, pe.n)], inc=ve, chain=False)
                    self._p7war_ve = ve.n
            vtm_ve = ve.n
            for qi in range(4):
                qt = 4 * tc + qi
                kts = [k for k in (qt - 2, qt - 1, qt) if k >= 0]
                for i, kt in enumerate(kts):
                    slot = kt % 8
                    w0 = [(sc, qkv_sc)]
                    if i == 0:
                        if s3_war_ve:
                            w0.append((ve, s3_war_ve))
                        if s3_war_sc:
                            w0.append((sc, s3_war_sc))
                    for dc in range(NDC):
                        p.mm(S3[:, i * 128:(i + 1) * 128],
                             kbuf[:, dc * 1024 + slot * 128: dc * 1024 + (slot + 1) * 128],
                             qbuf[:, dc * 512 + qi * 128: dc * 512 + (qi + 1) * 128],
                             dc == 0, dc == NDC - 1,
                             waits=(w0 if dc == 0 else []),
                             inc=pe if dc == NDC - 1 else None)
                smm = pe.n
                ev = []
                for i, kt in enumerate(kts):
                    sl = sT[:, i * 128:(i + 1) * 128]
                    if kt == qt:
                        p.tt(sl, S3[:, i * 128:(i + 1) * 128], self.m128ge[:],
                             ALU.mult, waits=[(pe, smm)], inc=ve)
                        ev.append((ve, ve.n))
                        s3_war_ve = ve.n
                    elif kt == qt - 2:
                        p.tt(sl, S3[:, i * 128:(i + 1) * 128], self.m128lt[:],
                             ALU.mult, waits=[(pe, smm)], inc=ve)
                        ev.append((ve, ve.n))
                        s3_war_ve = ve.n
                    else:
                        p.vcopy(sl, S3[:, i * 128:(i + 1) * 128],
                                waits=[(pe, smm)], inc=ve)
                        ev.append((ve, ve.n))
                        s3_war_ve = ve.n
                for dc in range(NDC):
                    obank = Oa if dc < 4 else Ob
                    col = (dc % 4) * 128
                    for i, kt in enumerate(kts):
                        slot = kt % 8
                        w2 = list(ev) + [(ve, vtm_ve)]
                        if i == 0 and dc == 0 and o_war:
                            w2.append((ve, o_war))
                        p.mm(obank[:, col:col + 128],
                             vtm[:, (slot * NDC + dc) * 128:(slot * NDC + dc + 1) * 128],
                             sT[:, i * 128:(i + 1) * 128],
                             i == 0, i == len(kts) - 1,
                             waits=w2,
                             inc=pe if i == len(kts) - 1 else None)
                av_done = pe.n
                for dc in range(NDC):
                    obank = Oa if dc < 4 else Ob
                    col = (dc % 4) * 128
                    p.vcopy(osb4[:, dc * 512 + qi * 128: dc * 512 + (qi + 1) * 128],
                            obank[:, col:col + 128],
                            waits=[(pe, av_done)] if dc == 0 else (), inc=ve)
                o_war = ve.n
            osb_ve = ve.n
            ab2 = self.pW0[:, 0:512]
            bb2 = self.pW0[:, 512:1024]
            bc2 = self.emit_stats(
                lambda i, b=osb4: b[:, i * 512:(i + 1) * 512], 512, NDC,
                ab2, bb2, [(ve, osb_ve)],
                war_bank=[(ve, post_apply_ve), (sc, qkv_sc)],
                cast_bufs=(xb16, sqb16))
            p.vcopy(abS[:], ab2, waits=[(pe, bc2), (ve, post_apply_ve)], inc=ve)
            p.vcopy(bbS[:], bb2, inc=ve)
            ab_copied = ve.n

            def flush_post(x2c=x2col, tcc=tc, abc=ab_copied):
                for dc in range(NDC):
                    tm = self.sq[dc % 2][:, 0:512]
                    w2 = [(ve, abc)]
                    if dc < 2 and self.sq_war:
                        w2.append((pe, self.sq_war))
                    p.tt(tm, osb4[:, dc * 512:(dc + 1) * 512], abS[:], ALU.mult,
                         waits=w2, inc=ve)
                    p.tt(tm, tm, bbS[:], ALU.add, inc=ve)
                    p.ts(tm, tm, self.lnw_sb[:, dc:dc + 1],
                         self.lnb_sb[:, dc:dc + 1], ALU.mult, ALU.add, inc=ve)
                    xsl = x2c[:, dc * 512:(dc + 1) * 512]
                    p.tt(xsl, xsl, tm, ALU.add, inc=ve)
                self.sq_ve_war = ve.n
                gds = [gd, self.gdB][tcc % 2]
                p.dma("gpsimd", dst_w[:, :, tcc * 512:(tcc + 1) * 512],
                      x2c.ap().rearrange("p (c j) -> p c j", c=NDC),
                      waits=[(ve, ve.n)], inc=gds)
                store_done[tcc] = (gds, gds.n)
                return ve.n

            pending_post = flush_post
            if 4 in self.phases:
                self._p4u_step()
        if pending_post is not None:
            post_apply_ve = pending_post()
        es3.close()
        return max(gd.n, 1)

    def _p4u_alloc(self):
        nc = self.nc
        self.p4u_es = ExitStack()
        self.p4u_wt = self.p4u_es.enter_context(
            nc.sbuf_tensor("p4_wt", [128, NDC * D], BF16))
        self.p4u_stg = ExitStack()
        sbs = lambda n, s, t: self.p4u_stg.enter_context(nc.sbuf_tensor(n, s, t))
        self.p4u_pkt = [sbs(f"p4_pk{i}", [128, 256], I32) for i in range(2)]
        self.p4u_ei = sbs("p4_ei", [128, 256], I32)
        self.p4u_wrow = [sbs(f"p4_wr{i}", [128, D], F32) for i in range(2)]
        self.p4u_wrow_done = {}
        self.p4u_ot = 0
        self.p4u_pe_war = 0
        self._p7war_ve = getattr(self, "_p7war_ve", 0)

    def _p4u_step(self):
        if self.p4u_ot >= NDC:
            return
        ot = self.p4u_ot
        self.p4u_ot += 1
        p, d = self.plan, self.d
        ve, pe = self.ve, self.pe
        buf = self.p4u_pkt[ot % 2]
        lws = [self.lwA, self.lwB][ot % 2]
        w = [(ve, self.p4u_wrow_done[ot - 2])] \
            if ot - 2 in self.p4u_wrow_done else []
        p.dma("sync", buf[:], d["pkd"][ot * 128:(ot + 1) * 128, :],
              waits=w, inc=lws)
        nld = lws.n
        wr = self.p4u_wrow[ot % 2]
        ei = self.p4u_ei
        wrw = [(pe, self.p4u_pe_war)] if self.p4u_pe_war else []
        for j in range(4):
            if j == 0:
                p.ts(ei[:], buf[:], 3, None, ALU.bitwise_and,
                     waits=[(lws, nld)] + wrw, inc=ve)
            else:
                p.ts(ei[:], buf[:], 2 * j, None, ALU.logical_shift_right,
                     inc=ve)
                p.ts(ei[:], ei[:], 3, None, ALU.bitwise_and, inc=ve)
            p.ts(wr[:, j:D:4], ei[:], -1.0, None, ALU.add, inc=ve)
        wrd = ve.n
        Tb = self.p7[:, 0:128]
        for ic in range(NDC):
            p.trans(Tb, wr[:, ic * 128:(ic + 1) * 128], self.ident_f32[:],
                    waits=[(ve, wrd), (ve, self._p7war_ve)], inc=pe)
            self.p4u_pe_war = pe.n
            p.vcopy(self.p4u_wt[:, ic * D + ot * 128: ic * D + (ot + 1) * 128],
                    Tb, waits=[(pe, pe.n)], inc=ve, chain=False)
            self._p7war_ve = ve.n
        self.p4u_wrow_done[ot] = ve.n
        self.p4u_wt_ready = ve.n

    def emit_p4(self):
        nc, p, d = self.nc, self.plan, self.d
        gd, sc, ve, pe = self.gd, self.sc, self.ve, self.pe
        lw = [self.lwA, self.lwB]
        lx = self.lxA
        src, dst = self.act_io(4)
        es4 = ExitStack()
        sb = lambda name, shape, dt: es4.enter_context(nc.sbuf_tensor(name, shape, dt))
        if not hasattr(self, "p4u_ot"):
            self._p4u_alloc()
        while self.p4u_ot < NDC:
            self._p4u_step()
        wt = self.p4u_wt
        wt_ready = self.p4u_wt_ready
        wt_pe = self.p4u_pe_war
        self.p4u_stg.close()

        x3col2 = [sb(f"p4_x3col{i}", [128, T], F32) for i in range(2)]
        xn3 = sb("p4_xn3", [128, T], F32)
        xtm = sb("p4_xtm", [128, T], F32)
        qfm = sb("p4_qfm", [128, T], BF16)
        xqb = [sb(f"p4_xqb{i}", [128, 1024], BF16) for i in range(2)]
        xb16 = [sb(f"p4_xb16{i}", [128, 512], BF16) for i in range(2)]
        qint = sb("p4_qint", [128, 1024], I32)
        sqb16 = [sb(f"p4_sqb16{i}", [128, 512], BF16) for i in range(2)]
        mcolp = [sb(f"p4_mcol{i}", [128, 1], F32) for i in range(2)]
        mtmpp = [sb(f"p4_mcol2{i}", [128, 1], F32) for i in range(2)]
        ascolp = [sb(f"p4_ascol{i}", [128, 1], F32) for i in range(2)]
        invcolp = [sb(f"p4_invcol{i}", [128, 1], F32) for i in range(2)]

        ab = self.pW0[:, 0:512]
        bb = self.pW0[:, 512:1024]
        Tf = [self.pW0[:, 1024:1152], self.pW1[:, 1024:1152]]
        Tfb = [self.pW0.bitcast(BF16)[:, 2048:2176],
               self.pW1.bitcast(BF16)[:, 2048:2176]]
        yb = [self.p6[:, 0:512], self.pW1[:, 0:512]]

        src_r = src.rearrange("(c p) t -> p c t", p=128)
        dst_w = dst.rearrange("(c p) t -> p c t", p=128)

        store_done = {}
        apply_done = self.p6_setup_war
        tf_war = [0, 0]
        epi_done = {}
        for tc in range(8):
            x3col = x3col2[tc % 2]
            lds = [self.lxA, self.lxB][tc % 2]
            w = [store_done[tc - 2]] if tc - 2 in store_done else []
            if tc == 0:
                w += [(ve, wt_ready), (pe, wt_pe)]
            p.dma("sync", x3col.ap().rearrange("p (c j) -> p c j", c=NDC),
                  src_r[:, :, tc * 512:(tc + 1) * 512], waits=w, inc=lds)
            nld = lds.n
            bc = self.emit_stats(
                lambda i, b=x3col: b[:, i * 512:(i + 1) * 512], 512, NDC,
                ab, bb, [(lds, nld)],
                war_bank=[(ve, apply_done)], cast_bufs=(xb16, sqb16))
            apply_done = self.emit_apply(
                lambda dc, b=x3col: b[:, dc * 512:(dc + 1) * 512],
                lambda dc: xn3[:, dc * 512:(dc + 1) * 512],
                512, NDC, ab, bb, bc)
            adc = list(self.apply_dc)
            for ti in range(4):
                for dc in range(NDC):
                    p.trans(Tf[dc % 2],
                            xn3[:, dc * 512 + ti * 128: dc * 512 + (ti + 1) * 128],
                            self.ident_f32[:],
                            waits=[(ve, adc[dc]), (ve, tf_war[dc % 2])], inc=pe)
                    p.vcopy(xtm[:, ti * 1024 + dc * 128: ti * 1024 + (dc + 1) * 128],
                            Tf[dc % 2], waits=[(pe, pe.n)], inc=ve)
                    tf_war[dc % 2] = ve.n
            qdone = {}
            xqb_pe = getattr(self, "_p4_xqb_pe", [0, 0])

            def bwd(ti):
                xq = xqb[ti % 2]
                for dc in range(NDC):
                    p.trans(Tfb[dc % 2], xq[:, dc * 128:(dc + 1) * 128],
                            self.ident_bf[:],
                            waits=[(ve, qdone[ti]), (ve, tf_war[dc % 2])],
                            inc=pe)
                    xqb_pe[ti % 2] = pe.n
                    p.vcopy(qfm[:, dc * 512 + ti * 128: dc * 512 + (ti + 1) * 128],
                            Tfb[dc % 2], waits=[(pe, pe.n)], inc=ve,
                            chain=False)
                    tf_war[dc % 2] = ve.n

            asdone = {}

            def qA(ti):
                tmi = xtm[:, ti * 1024:(ti + 1) * 1024]
                mc, iv = mcolp[ti % 2], invcolp[ti % 2]
                p.vmax(mc[:], tmi, True, inc=ve)
                p.ts(mc[:], mc[:], 1e-5, None, ALU.max, inc=ve)
                p.ts(iv[:], mc[:], 1.0 / 127.0, None, ALU.mult, inc=ve)
                inv_ve = ve.n
                p.act(mtmpp[ti % 2][:], iv[:], AF.Ln, waits=[(ve, inv_ve)],
                      inc=sc)
                p.act(ascolp[ti % 2][:], mtmpp[ti % 2][:], AF.Exp, scale=-1.0,
                      inc=sc)
                asdone[ti] = sc.n

            def qC(ti):
                tmi = xtm[:, ti * 1024:(ti + 1) * 1024]
                p.ts(qint[:], tmi, ascolp[ti % 2][:], -128.0, ALU.mult,
                     ALU.max, waits=[(sc, asdone[ti])], inc=ve)
                p.ts(xqb[ti % 2][:], qint[:], 127.0, invcolp[ti % 2][:],
                     ALU.min, ALU.mult,
                     waits=[(pe, xqb_pe[ti % 2])] if xqb_pe[ti % 2] else [],
                     inc=ve)
                qdone[ti] = ve.n

            for ti in range(4):
                qA(ti)
                if ti >= 1:
                    qC(ti - 1)
                if ti >= 2:
                    bwd(ti - 2)
            qC(3)
            bwd(2)
            bwd(3)
            self._p4_xqb_pe = xqb_pe
            q_ready = ve.n
            for ot in range(NDC):
                ybk = yb[ot % 2]
                pot = ot - 2 if ot >= 2 else (6 + ot if tc > 0 else None)
                war = [(sc, epi_done[pot])] if pot in epi_done else []
                if ot == 0 and tc == 0:
                    war.append((ve, self.p6_setup_war))
                for ic in range(NDC):
                    p.mm(ybk, wt[:, ic * D + ot * 128: ic * D + (ot + 1) * 128],
                         qfm[:, ic * 512:(ic + 1) * 512], ic == 0, ic == NDC - 1,
                         waits=[(ve, q_ready), (ve, wt_ready)] +
                         (war if ic == 0 else []),
                         inc=pe if ic == NDC - 1 else None)
                y_done = pe.n
                tm = self.sq[ot % 2][:, 0:512]
                wts = [(pe, y_done)] + war
                p.stt(tm, ybk, self.ws_col[:], x3col[:, ot * 512:(ot + 1) * 512],
                      ALU.mult, ALU.add, waits=wts, inc=ve)
                p.act(x3col[:, ot * 512:(ot + 1) * 512], tm, AF.Identity,
                      bias=self.bias1_sb[:, ot:ot + 1], waits=[(ve, ve.n)], inc=sc)
                epi_done[ot] = sc.n
            self.sq_ve_war = ve.n
            self.sq_war = 0
            last_sc = sc.n
            gds = [gd, self.gdB][tc % 2]
            p.dma("gpsimd", dst_w[:, :, tc * 512:(tc + 1) * 512],
                  x3col.ap().rearrange("p (c j) -> p c j", c=NDC),
                  waits=[(sc, last_sc)], inc=gds)
            store_done[tc] = (gds, gds.n)
        es4.close()
        self.p4u_es.close()
        return gd.n

    def build(self):
        self.declare_dram()
        self.alloc_persistent()
        self.emit_setup()
        emitters = {1: self.emit_p1, 2: self.emit_p2, 3: self.emit_p3,
                    4: self.emit_p4}
        for ph in self.phases:
            final_gd = emitters[ph]()
            self.plan.barrier([(self.gd, self.gd.n), (self.gdB, self.gdB.n),
                               (self.ve, self.ve.n),
                               (self.pe, self.pe.n), (self.sc, self.sc.n),
                               (self.lxA, self.lxA.n), (self.lxB, self.lxB.n),
                               (self.lwA, self.lwB.n and self.lwA.n or self.lwA.n),
                               (self.lwB, self.lwB.n)])
        self.plan.wait_only("gpsimd", [(self.gd, self.gd.n),
                                       (self.gdB, self.gdB.n)])
        block = self.es.enter_context(self.nc.Block())
        self.plan.realize(block)
        self.es.close()
        return self.nc


_NC_CACHE = {}


def _get_nc(phases):
    key = tuple(phases)
    if key not in _NC_CACHE:
        _NC_CACHE[key] = Builder(key).build()
    return _NC_CACHE[key]


def _stage_consts(inputs):
    f32 = np.float32
    g = lambda k: np.asarray(inputs[k])
    c = {}
    c["bvq"] = np.ascontiguousarray(
        g("L0_attn_bv_q").astype(f32).reshape(-1).reshape(8, 128).T)
    c["bvk"] = np.ascontiguousarray(
        g("L0_attn_bv_k").astype(f32).reshape(-1).reshape(8, 128).T)
    c["bvv"] = np.ascontiguousarray(
        g("L0_attn_bv_v").astype(f32).reshape(-1).reshape(8, 128).T)
    c["w0g"] = np.ascontiguousarray(g("L0_ffn_w_gate").astype(np.int32).T)
    c["w0u"] = np.ascontiguousarray(g("L0_ffn_w_up").astype(np.int32).T)
    c["w0d"] = np.ascontiguousarray(g("L0_ffn_w_down").astype(np.int32).T)
    c["wqkv"] = np.ascontiguousarray(g("L1_attn_W_qkv").astype(f32))
    c["bqkv"] = np.ascontiguousarray(
        g("L1_attn_b_qkv").astype(f32).reshape(24, 128).T)
    c["lnw"] = np.ascontiguousarray(
        g("L1_attn_ln_w").astype(f32).reshape(8, 128).T)
    c["lnb"] = np.ascontiguousarray(
        g("L1_attn_ln_b").astype(f32).reshape(8, 128).T)
    c["pkd"] = np.ascontiguousarray(
        g("L1_ffn_packed_weight").astype(np.int32).reshape(1024, 256))
    c["wsc"] = np.ascontiguousarray(
        g("L1_ffn_weight_scale").astype(f32).reshape(1, 1))
    c["bias1"] = np.ascontiguousarray(
        g("L1_ffn_bias").astype(f32).reshape(8, 128).T)
    return c


def kernel(**inputs):
    x = np.asarray(inputs["L0_attn_x"]).astype(np.float32)
    consts = _stage_consts(inputs)
    cur = [np.ascontiguousarray(x[b].T) for b in range(8)]
    for phases in PHASE_GROUPS:
        nc = _get_nc(phases)
        in_maps = [dict(consts, XIN=cur[b]) for b in range(8)]
        res = bass_utils.run_bass_kernel_spmd(nc, in_maps, core_ids=list(range(8)))
        cur = [np.asarray(res.results[b]["XOUT"]) for b in range(8)]
    out = np.stack([cur[b].T for b in range(8)], axis=0)
    return out.astype(np.float32)


# version 23 (final)
# speedup vs baseline: 1.0071x; 1.0071x over previous
import numpy as np
import concourse.bass as bass
import concourse.mybir as mybir
import concourse.bass_utils as bass_utils
from contextlib import ExitStack

F32 = mybir.dt.float32
BF16 = mybir.dt.bfloat16
I32 = mybir.dt.int32
AF = mybir.ActivationFunctionType
ALU = mybir.AluOpType
AX = mybir.AxisListType

T = 4096
D = 1024
HID = 3072
NDC = 8
MAGIC = 12582912.0
EPS = 1e-5

ENGINES = ("sync", "scalar", "vector", "tensor", "gpsimd")

PHASE_GROUPS = [(1, 2, 3, 4)]



class S:
    __slots__ = ("h", "n", "name")

    def __init__(self, name):
        self.h = None
        self.n = 0
        self.name = name


class Plan:
    def __init__(self):
        self.ops = {e: [] for e in ENGINES}
        self.lw = {e: {} for e in ENGINES}
        self.pending = {e: [] for e in ENGINES}
        self.engsem = {}

    def barrier(self, waits):
        for e in ENGINES:
            self.pending[e].extend(waits)

    def emit(self, eng, fn, waits=(), inc=None, by=1, chain=True):
        raw = list(self.pending[eng]) + list(waits)
        self.pending[eng] = []
        s_own = self.engsem.get(eng)
        if s_own is not None:
            if inc is None:
                inc = s_own
            if chain and s_own.n > 0:
                raw.append((s_own, s_own.n))
        wl = []
        for s, v in raw:
            if v <= 0:
                continue
            k = id(s)
            if self.lw[eng].get(k, 0) < v:
                wl.append((s, v))
                self.lw[eng][k] = v
        self.ops[eng].append((fn, tuple(wl), inc, by))
        if inc is not None:
            inc.n += by
            return inc.n
        return 0

    def dma(self, eng, out, in_, waits=(), inc=None):
        return self.emit(
            eng, lambda e, o=out, i=in_: e.dma_start(out=o, in_=i), waits, inc, 16)

    def mm(self, out, lhsT, rhs, start, stop, waits=(), inc=None, tp=None):
        if tp is None:
            fn = lambda e, o=out, l=lhsT, r=rhs, s1=start, s2=stop: e.matmul(
                o, l, r, start=s1, stop=s2)
        else:
            fn = lambda e, o=out, l=lhsT, r=rhs, s1=start, s2=stop, t=tp: e.matmul(
                o, l, r, start=s1, stop=s2, tile_position=t)
        return self.emit("tensor", fn, waits, inc)

    def trans(self, out, in_, ident, waits=(), inc=None):
        return self.emit(
            "tensor",
            lambda e, o=out, i=in_, d=ident: e.transpose(o, i, d), waits, inc)

    def act(self, out, in_, func, waits=(), inc=None, bias=None, scale=None):
        kw = {}
        if bias is not None:
            kw["bias"] = bias
        if scale is not None:
            kw["scale"] = scale
        return self.emit(
            "scalar",
            lambda e, o=out, i=in_, f=func, k=kw: e.activation(
                out=o, in_=i, func=f, **k), waits, inc)

    def scopy(self, out, in_, waits=(), inc=None):
        return self.emit(
            "scalar", lambda e, o=out, i=in_: e.copy(out=o, in_=i), waits, inc)

    def ts(self, out, in0, s1, s2, op0, op1=None, waits=(), inc=None,
           chain=True):
        if op1 is None:
            fn = lambda e, o=out, i=in0, a=s1, p0=op0: e.tensor_scalar(
                out=o, in0=i, scalar1=a, scalar2=None, op0=p0)
        else:
            fn = lambda e, o=out, i=in0, a=s1, b=s2, p0=op0, p1=op1: e.tensor_scalar(
                out=o, in0=i, scalar1=a, scalar2=b, op0=p0, op1=p1)
        return self.emit("vector", fn, waits, inc, chain=chain)

    def stt(self, out, in0, scalar, in1, op0, op1, waits=(), inc=None):
        return self.emit(
            "vector",
            lambda e, o=out, i=in0, s=scalar, j=in1, p0=op0, p1=op1:
            e.scalar_tensor_tensor(
                out=o, in0=i, scalar=s, in1=j, op0=p0, op1=p1), waits, inc)

    def tt(self, out, in0, in1, op, waits=(), inc=None, chain=True):
        return self.emit(
            "vector",
            lambda e, o=out, a=in0, b=in1, p=op: e.tensor_tensor(
                out=o, in0=a, in1=b, op=p), waits, inc, chain=chain)

    def vcopy(self, out, in_, waits=(), inc=None, chain=True):
        return self.emit(
            "vector", lambda e, o=out, i=in_: e.tensor_copy(out=o, in_=i),
            waits, inc, chain=chain)

    def vmax(self, out, in_, absval, waits=(), inc=None):
        return self.emit(
            "vector",
            lambda e, o=out, i=in_, a=absval: e.reduce_max(
                out=o, in_=i, axis=AX.X, apply_absolute_value=a), waits, inc)

    def vrecip(self, out, in_, waits=(), inc=None):
        return self.emit(
            "vector", lambda e, o=out, i=in_: e.reciprocal(out=o, in_=i),
            waits, inc)

    def memset(self, ap, v, waits=(), inc=None, eng="vector"):
        return self.emit(
            eng, lambda e, a=ap, x=v: e.memset(a, x), waits, inc)

    def iota(self, out, pattern, chm, waits=(), inc=None):
        return self.emit(
            "gpsimd",
            lambda e, o=out, p=pattern, c=chm: e.iota(
                o, pattern=p, base=0, channel_multiplier=c,
                allow_small_or_imprecise_dtypes=True), waits, inc)

    def wait_only(self, eng, waits):
        self.ops[eng].append((None, tuple(waits), None, 1))

    def realize(self, block):
        for eng in ENGINES:
            ops = self.ops[eng]
            if not ops:
                continue

            def section(e, _ops=ops):
                for fn, wl, inc, by in _ops:
                    for s, v in wl:
                        e.wait_ge(s.h, v)
                    if fn is None:
                        continue
                    ins = fn(e)
                    if inc is not None:
                        ins.then_inc(inc.h, by)

            getattr(block, eng)(section)


class Builder:

    def __init__(self, phases):
        self.phases = tuple(phases)
        self.nc = bass.Bass("TRN2", target_bir_lowering=False)
        self.plan = Plan()
        self.es = ExitStack()
        self.sems = {}
        for name in ("c0", "lxA", "lxB", "lxC", "lxD", "lwA", "lwB", "lwC",
                     "lwD", "gd", "gdB", "sc", "ve", "pe", "gp"):
            s = S(name)
            self.sems[name] = s
            setattr(self, name, s)
        self.sq_war = 0
        self.sq_sc_war = 0

    def declare_dram(self):
        nc = self.nc
        d = {}
        d["XIN"] = nc.dram_tensor("XIN", [D, T], F32, kind="ExternalInput").ap()
        d["XOUT"] = nc.dram_tensor("XOUT", [D, T], F32, kind="ExternalOutput").ap()
        ph = self.phases
        for i in range(len(ph) - 1):
            name = f"SCR{ph[i]}"
            d[name] = nc.dram_tensor(name, [D, T], F32, kind="Internal").ap()
        d["bvq"] = nc.dram_tensor("bvq", [128, 8], F32, kind="ExternalInput").ap()
        d["bvk"] = nc.dram_tensor("bvk", [128, 8], F32, kind="ExternalInput").ap()
        d["bvv"] = nc.dram_tensor("bvv", [128, 8], F32, kind="ExternalInput").ap()
        d["w0g"] = nc.dram_tensor("w0g", [D, HID], I32, kind="ExternalInput").ap()
        d["w0u"] = nc.dram_tensor("w0u", [D, HID], I32, kind="ExternalInput").ap()
        d["w0d"] = nc.dram_tensor("w0d", [HID, D], I32, kind="ExternalInput").ap()
        d["wqkv"] = nc.dram_tensor("wqkv", [D, 3 * D], F32, kind="ExternalInput").ap()
        d["bqkv"] = nc.dram_tensor("bqkv", [128, 24], F32, kind="ExternalInput").ap()
        d["lnw"] = nc.dram_tensor("lnw", [128, 8], F32, kind="ExternalInput").ap()
        d["lnb"] = nc.dram_tensor("lnb", [128, 8], F32, kind="ExternalInput").ap()
        d["pkd"] = nc.dram_tensor("pkd", [D, 256], I32, kind="ExternalInput").ap()
        d["wsc"] = nc.dram_tensor("wsc", [1, 1], F32, kind="ExternalInput").ap()
        d["bias1"] = nc.dram_tensor("bias1", [128, 8], F32, kind="ExternalInput").ap()
        self.d = d

    def act_io(self, phase):
        ph = self.phases
        i = ph.index(phase)
        src = self.d["XIN"] if i == 0 else self.d[f"SCR{ph[i - 1]}"]
        dst = self.d["XOUT"] if i == len(ph) - 1 else self.d[f"SCR{phase}"]
        return src, dst

    def alloc_persistent(self):
        nc, es = self.nc, self.es
        sb = lambda name, shape, dt: es.enter_context(nc.sbuf_tensor(name, shape, dt))
        self.ones_col = sb("ones_col", [128, 1], F32)
        self.ones_b16 = sb("ones_b16", [128, 1], BF16)
        self.ones_row = sb("ones_row", [1, 128], F32)
        self.iot128 = sb("iot128", [128, 128], F32)
        self.m128ge = sb("m128ge", [128, 128], BF16)
        self.m128lt = sb("m128lt", [128, 128], BF16)
        self.ident_f32 = sb("ident_f32", [128, 128], F32)
        self.ident_bf = sb("ident_bf", [128, 128], BF16)
        self.sgq = sb("sgq", [128, 8], F32)
        self.sgk = sb("sgk", [128, 8], F32)
        self.sgv = sb("sgv", [128, 8], F32)
        self.bq_all = sb("bq_all", [128, 24], F32)
        self.bq_s = sb("bq_s", [128, 8], F32)
        self.lnw_sb = sb("lnw_sb", [128, 8], F32)
        self.lnb_sb = sb("lnb_sb", [128, 8], F32)
        self.bias1_sb = sb("bias1_sb", [128, 8], F32)
        self.wsc_sb = sb("wsc_sb", [1, 1], F32)
        self.ws_col = sb("ws_col", [128, 1], F32)
        self.r_mean = sb("r_mean", [1, 512], F32)
        self.r_var = sb("r_var", [1, 512], F32)
        self.sq = [sb(f"sqb{i}", [128, 512], F32) for i in range(2)]
        self.sq_ve_war = 0
        ps = lambda name, shape, dt: es.enter_context(nc.psum_tensor(name, shape, dt))
        self.pW0 = ps("pW0", [128, 1536], F32)
        self.pW1 = ps("pW1", [128, 1536], F32)
        self.p6 = ps("p6", [128, 512], F32)
        self.p7 = ps("p7", [128, 512], F32)
        for s in self.sems.values():
            s.h = es.enter_context(nc.semaphore(s.name))
        self.plan.engsem = {"vector": self.ve, "scalar": self.sc,
                            "gpsimd": self.gp}

    def emit_setup(self):
        p, d = self.plan, self.d
        ld, sc, ve, pe, gp = self.c0, self.sc, self.ve, self.pe, self.gp
        p.memset(self.ones_col[:], 1.0, inc=ve)
        p.memset(self.ones_b16[:], 1.0, inc=ve)
        p.memset(self.ones_row[:], 1.0, inc=ve)
        ones_ve = ve.n
        p.iota(self.iot128[:], [[1, 128]], -1, inc=gp)
        g2 = gp.n
        p.ts(self.m128ge[:], self.iot128[:], 0.0, None, ALU.is_ge,
             waits=[(gp, g2)], inc=ve)
        p.ts(self.m128lt[:], self.iot128[:], -1.0, None, ALU.is_le, inc=ve)
        p.ts(self.ident_f32[:], self.iot128[:], 0.0, None, ALU.is_equal, inc=ve)
        p.ts(self.ident_bf[:], self.iot128[:], 0.0, None, ALU.is_equal, inc=ve)
        p.dma("sync", self.sgq[:], d["bvq"][:], inc=ld)
        p.dma("sync", self.sgk[:], d["bvk"][:], inc=ld)
        p.dma("sync", self.sgv[:], d["bvv"][:], inc=ld)
        p.dma("sync", self.bq_all[:], d["bqkv"][:], inc=ld)
        p.dma("sync", self.lnw_sb[:], d["lnw"][:], inc=ld)
        p.dma("sync", self.lnb_sb[:], d["lnb"][:], inc=ld)
        p.dma("sync", self.bias1_sb[:], d["bias1"][:], inc=ld)
        p.dma("sync", self.wsc_sb[:], d["wsc"][:], inc=ld)
        n_ld = ld.n
        p.act(self.sgq[:], self.sgq[:], AF.Sign, waits=[(ld, n_ld)], inc=sc)
        p.act(self.sgk[:], self.sgk[:], AF.Sign, inc=sc)
        p.act(self.sgv[:], self.sgv[:], AF.Sign, inc=sc)
        n_sc = sc.n
        p.tt(self.sgq[:], self.sgq[:], self.sgk[:], ALU.mult,
             waits=[(sc, n_sc)], inc=ve)
        p.ts(self.bq_s[:], self.bq_all[:, 0:8], 1.0 / 512.0, None, ALU.mult,
             waits=[(ld, n_ld)], inc=ve)
        p.mm(self.p6[:, 0:1], self.ones_row[:], self.wsc_sb[:],
             True, True, waits=[(ld, n_ld), (ve, ones_ve)], inc=pe)
        p.vcopy(self.ws_col[:], self.p6[:, 0:1], waits=[(pe, pe.n)], inc=ve)
        self.setup_ve = ve.n
        self.p6_setup_war = ve.n

    def emit_stats(self, get_blk, width, n_blk, ab, bb, wait_x, war_bank=(),
                   cast_bufs=None):
        p = self.plan
        sc, ve, pe = self.sc, self.ve, self.pe
        sum_row = ab[0:1, 0:width]
        ss_row = bb[0:1, 0:width]
        n = 128 * n_blk
        ss_cnt = {}
        for i in range(n_blk):
            blk = get_blk(i)
            w = list(wait_x)
            if i - 2 in ss_cnt:
                w.append((pe, ss_cnt[i - 2]))
            elif self.sq_war:
                w.append((pe, self.sq_war))
            if i < 2 and self.sq_ve_war:
                w.append((ve, self.sq_ve_war))
            if cast_bufs is None:
                sqb = self.sq[i % 2][:, 0:width]
                p.act(sqb, blk, AF.Square, waits=w, inc=sc)
                sq_done = sc.n
                p.mm(sum_row, self.ones_col[:], blk, i == 0, i == n_blk - 1,
                     waits=list(wait_x) + (list(war_bank) if i == 0 else []),
                     inc=pe)
                p.mm(ss_row, self.ones_col[:], sqb, i == 0, i == n_blk - 1,
                     waits=[(sc, sq_done)], inc=pe)
            else:
                xb = cast_bufs[0][i % 2][:, 0:width]
                sqb = cast_bufs[1][i % 2][:, 0:width]
                p.vcopy(xb, blk, waits=w, inc=ve)
                xb_done = ve.n
                p.act(sqb, blk, AF.Square, waits=w, inc=sc)
                sq_done = sc.n
                p.mm(sum_row, self.ones_b16[:], xb, i == 0, i == n_blk - 1,
                     waits=[(ve, xb_done)] +
                     (list(war_bank) if i == 0 else []), inc=pe)
                p.mm(ss_row, self.ones_b16[:], sqb, i == 0, i == n_blk - 1,
                     waits=[(sc, sq_done)], inc=pe)
            ss_cnt[i] = pe.n
        self.sq_war = pe.n
        mm_done = pe.n
        rm = self.r_mean[:, 0:width]
        rv = self.r_var[:, 0:width]
        p.ts(rm, sum_row, 1.0 / n, None, ALU.mult, waits=[(pe, mm_done)], inc=ve)
        p.tt(rv, rm, rm, ALU.mult, inc=ve)
        p.stt(rv, ss_row, 1.0 / n, rv, ALU.mult, ALU.subtract, inc=ve)
        p.ts(rv, rv, EPS, None, ALU.add, inc=ve)
        rows_ve = ve.n
        p.act(rv, rv, AF.Ln, waits=[(ve, rows_ve)], inc=sc)
        p.act(rv, rv, AF.Exp, scale=-0.5, inc=sc)
        a_sc = sc.n
        p.stt(rm, rm, -1.0, rv, ALU.mult, ALU.mult,
              waits=[(sc, a_sc)], inc=ve)
        b_ve = ve.n
        p.mm(ab, self.ones_row[:], rv, True, True,
             waits=[(sc, a_sc)], inc=pe)
        p.mm(bb, self.ones_row[:], rm, True, True, waits=[(ve, b_ve)], inc=pe)
        return pe.n

    def emit_apply(self, get_xblk, get_out, width, n_blk, ab, bb, bc):
        p = self.plan
        ve, pe = self.ve, self.pe
        self.apply_dc = []
        for dc in range(n_blk):
            tm = self.sq[dc % 2][:, 0:width]
            w = [(pe, bc)]
            if dc < 2 and self.sq_war:
                w.append((pe, self.sq_war))
            p.tt(tm, get_xblk(dc), ab, ALU.mult, waits=w, inc=ve)
            p.tt(get_out(dc), tm, bb, ALU.add, inc=ve)
            self.apply_dc.append(ve.n)
        self.sq_ve_war = ve.n
        return ve.n

    def emit_p1(self):
        nc, p, d = self.nc, self.plan, self.d
        gd, sc, ve, pe = self.gd, self.sc, self.ve, self.pe
        lx = [self.lxA, self.lxB]
        src, dst = self.act_io(1)
        es1 = ExitStack()
        sb = lambda name, shape, dt: es1.enter_context(nc.sbuf_tensor(name, shape, dt))
        xn = sb("p1_xn", [128, NDC * T], BF16)
        xcol = [sb(f"p1_xcol{i}", [128, T], F32) for i in range(2)]
        xb16 = [sb(f"p1_xb16{i}", [128, 512], BF16) for i in range(2)]
        sqb16 = [sb(f"p1_sqb16{i}", [128, 512], BF16) for i in range(2)]
        iot512 = sb("p1_iot512", [128, 512], F32)
        m512 = sb("p1_m512", [128, 4 * 512], BF16)
        vtmp = [sb(f"p1_vtmp{i}", [128, 128], F32) for i in range(2)]
        src_r = src.rearrange("(c p) t -> p c t", p=128)
        p.iota(iot512[:], [[1, 512]], -1, inc=self.gp)
        g2 = self.gp.n
        for q in range(4):
            p.ts(m512[:, q * 512:(q + 1) * 512], iot512[:],
                 128.0 * q, None, ALU.is_ge, waits=[(self.gp, g2)], inc=ve)

        abL = [self.pW0[:, 0:512], self.pW1[:, 0:512]]
        bbL = [self.pW0[:, 512:1024], self.pW1[:, 512:1024]]
        apply_done = {}
        for tc in range(8):
            buf = xcol[tc % 2]
            lds = lx[tc % 2]
            ab = abL[tc % 2]
            bb = bbL[tc % 2]
            w = [(ve, apply_done[tc - 2])] if tc - 2 in apply_done else []
            br = buf.ap().rearrange("p (c j) -> p c j", c=NDC)
            p.dma("sync", br[:, 0:4], src_r[:, 0:4, tc * 512:(tc + 1) * 512],
                  waits=w, inc=lds)
            nld = lds.n
            ldg = [self.lwC, self.lwD][tc % 2]
            p.dma("gpsimd", br[:, 4:8], src_r[:, 4:8, tc * 512:(tc + 1) * 512],
                  waits=w, inc=ldg)
            nldg = ldg.n
            bc = self.emit_stats(
                lambda i, b=buf: b[:, i * 512:(i + 1) * 512], 512, NDC,
                ab, bb, [(lds, nld), (ldg, nldg)],
                war_bank=[(ve, apply_done[tc - 2])] if tc - 2 in apply_done
                else [], cast_bufs=(xb16, sqb16))
            self.emit_apply(
                lambda dc, b=buf: b[:, dc * 512:(dc + 1) * 512],
                lambda dc, t=tc: xn[:, dc * T + t * 512: dc * T + (t + 1) * 512],
                512, NDC, ab, bb, bc)
            apply_done[tc] = ve.n
        p1a_ve = ve.n

        xc = sb("p1_xc", [128, T], F32)
        x1c = sb("p1_x1c", [128, T], F32)
        kc = sb("p1_kc", [128, 32 * 128], BF16)
        vc = sb("p1_vc", [128, 32 * 128], BF16)
        at_e = [sb(f"p1_ate{i}", [128, 1536], BF16) for i in range(2)]
        at_o = [sb(f"p1_ato{i}", [128, 1536], BF16) for i in range(2)]
        O_e = self.p6
        O_o = self.p7
        TbV = [self.pW0[:, 1024:1152], self.pW1[:, 1024:1152]]
        TbVb = [self.pW0.bitcast(BF16)[:, 2048:2176],
                self.pW1.bitcast(BF16)[:, 2048:2176]]

        sig_e_cnt = [0]
        sig_o_cnt = [0]
        av_cnt = [0, 0]
        gg = 0
        evac_last_c = 0
        store_cnt = 0
        tb_evac = 0
        o_war_ve = self.p6_setup_war
        vtmp_war = 0

        for c in range(NDC):
            p.dma("sync", xc[:], src[c * 128:(c + 1) * 128, :],
                  waits=[(ve, evac_last_c)], inc=self.lxC)
            xc_ld = self.lxC.n
            x1c_war = store_cnt
            kbuild = {}
            vbuild = {}
            prev_c_pe = pe.n
            sig_war = []
            if sig_e_cnt[0]:
                sig_war.append((sc, sig_e_cnt[0]))
            if sig_o_cnt[0]:
                sig_war.append((sc, sig_o_cnt[0]))
            tev = {}
            for kt in range(32):
                p.ts(kc[:, kt * 128:(kt + 1) * 128],
                     xn[:, c * T + kt * 128: c * T + (kt + 1) * 128],
                     self.sgq[:, c:c + 1], None, ALU.mult,
                     waits=[(ve, p1a_ve), (pe, prev_c_pe)], inc=ve,
                     chain=(kt == 0))
                kbuild[kt] = ve.n
            for kt in range(32):
                p.ts(vc[:, kt * 128:(kt + 1) * 128],
                     xn[:, c * T + kt * 128: c * T + (kt + 1) * 128],
                     self.sgv[:, c:c + 1], None, ALU.mult,
                     waits=[(ve, p1a_ve)], inc=ve, chain=(kt == 0))
                vbuild[kt] = ve.n
            prem_ve = ve.n
            for kt in range(32):
                tw = [(ve, prem_ve)] + (sig_war if kt < 2 else [])
                if kt - 2 in tev:
                    tw.append((ve, vbuild[kt - 2]))
                else:
                    tw.append((ve, tb_evac))
                p.trans(TbVb[kt % 2], vc[:, kt * 128:(kt + 1) * 128],
                        self.ident_bf[:], waits=tw, inc=pe)
                tev[kt] = pe.n
                p.vcopy(vc[:, kt * 128:(kt + 1) * 128], TbVb[kt % 2],
                        waits=[(pe, tev[kt])], inc=ve, chain=False)
                vbuild[kt] = ve.n
            tb_evac = ve.n
            vtmp_war = pe.n
            for qs in range(8):
                nkt = 4 * qs + 4
                kts_all = list(range(nkt))[::-1]
                groups = [kts_all[i0:i0 + 3] for i0 in range(0, nkt, 3)]
                pending = None

                def emit_av(pend, nkt=nkt):
                    grp, sig_e, sig_o, maskv = pend
                    glen = len(grp)
                    for h, atp_pair, sig, Ob in ((0, at_e, sig_e, O_e),
                                                 (1, at_o, sig_o, O_o)):
                        base = 64 * h
                        atp = atp_pair[sig[0]]
                        for i, kt in enumerate(grp):
                            w = [(sc, sig[1]), (ve, vbuild[kt])]
                            if maskv:
                                w.append((ve, maskv))
                            if kt == nkt - 1:
                                w.append((ve, o_war_ve))
                            p.mm(Ob[base:base + 64, :],
                                 vc[:, kt * 128 + base: kt * 128 + base + 64],
                                 atp[:, i * 512:(i + 1) * 512],
                                 kt == nkt - 1, kt == 0,
                                 waits=w,
                                 inc=pe if i == glen - 1 else None,
                                 tp=(0, 64) if h == 1 else None)
                    av_cnt[sig_e[0]] = pe.n

                for gi, grp in enumerate(groups):
                    par = gg % 2
                    gg += 1
                    glen = len(grp)
                    for h, Sp, sig in ((0, self.pW0, sig_e_cnt),
                                       (1, self.pW1, sig_o_cnt)):
                        base = 64 * h
                        for i, kt in enumerate(grp):
                            p.mm(Sp[:, i * 512:(i + 1) * 512],
                                 kc[base:base + 64, kt * 128:(kt + 1) * 128],
                                 xn[base:base + 64,
                                    c * T + qs * 512: c * T + (qs + 1) * 512],
                                 True, True,
                                 waits=[(ve, kbuild[kt]), (sc, sig[0]),
                                        (ve, p1a_ve), (ve, tb_evac)],
                                 inc=pe)
                        sig.append(pe.n)
                    mm_e_done = sig_e_cnt.pop()
                    mm_o_done = sig_o_cnt.pop()
                    if pending is not None:
                        emit_av(pending)
                        pending = None
                    p.act(at_e[par][:, 0:glen * 512], self.pW0[:, 0:glen * 512],
                          AF.Sigmoid, scale=0.5,
                          waits=[(pe, mm_e_done), (pe, av_cnt[par])], inc=sc)
                    sig_e_cnt[0] = sc.n
                    p.act(at_o[par][:, 0:glen * 512], self.pW1[:, 0:glen * 512],
                          AF.Sigmoid, scale=0.5, waits=[(pe, mm_o_done)], inc=sc)
                    sig_o_cnt[0] = sc.n
                    mask_ve = 0
                    for i, kt in enumerate(grp):
                        pp = kt - 4 * qs
                        if pp >= 0:
                            for atp in (at_e[par], at_o[par]):
                                sl = atp[:, i * 512:(i + 1) * 512]
                                p.tt(sl, sl, m512[:, pp * 512:(pp + 1) * 512],
                                     ALU.mult, waits=[(sc, sig_o_cnt[0])], inc=ve)
                            mask_ve = ve.n
                    pending = (grp, (par, sig_e_cnt[0]), (par, sig_o_cnt[0]),
                               mask_ve)
                emit_av(pending)
                w = [(pe, pe.n), (self.lxC, xc_ld)]
                if x1c_war:
                    w.append((gd, x1c_war))
                p.tt(x1c[0:64, qs * 512:(qs + 1) * 512],
                     xc[0:64, qs * 512:(qs + 1) * 512],
                     O_e[0:64, :], ALU.add, waits=w, inc=ve)
                p.tt(x1c[64:128, qs * 512:(qs + 1) * 512],
                     xc[64:128, qs * 512:(qs + 1) * 512],
                     O_o[64:128, :], ALU.add, inc=ve)
                o_war_ve = ve.n
            evac_last_c = ve.n
            p.dma("gpsimd", dst[c * 128:(c + 1) * 128, :], x1c[:],
                  waits=[(ve, evac_last_c)], inc=gd)
            store_cnt = gd.n
        es1.close()
        return gd.n

    def emit_p2(self):
        nc, p, d = self.nc, self.plan, self.d
        gd, sc, ve, pe = self.gd, self.sc, self.ve, self.pe
        lw = [self.lwA, self.lwB]
        lx = self.lxA
        src, dst = self.act_io(2)
        W2 = 512
        NT2 = T // W2
        es2 = ExitStack()
        sb = lambda name, shape, dt: es2.enter_context(nc.sbuf_tensor(name, shape, dt))
        wg = sb("p2_wg", [128, NDC * HID], BF16)
        wu = sb("p2_wu", [128, NDC * HID], BF16)
        wd = sb("p2_wd", [128, 24 * D], BF16)
        x1col = sb("p2_x1col", [128, T], F32)
        src_r0 = src.rearrange("(c p) t -> p c t", p=128)
        p.dma("sync", x1col.ap().rearrange("p (c j) -> p c j", c=NDC),
              src_r0[:, :, 0:W2], inc=self.lxA)
        tc0_ld = self.lxA.n
        xres = [sb(f"p2_xres{i}", [128, 512], F32) for i in range(2)]
        esw = ExitStack()
        wload = [esw.enter_context(
            nc.sbuf_tensor(f"p2_wl{i}", [128, HID // 2], I32)) for i in range(2)]
        cvt_done = {}
        pend_scale = {}
        scale_ve = {}
        w_ready = {}
        gidx = 0
        for mat, (wsb, wt, nch, chw) in enumerate(
                [(wg, d["w0g"], NDC, HID), (wu, d["w0u"], NDC, HID),
                 (wd, d["w0d"], 24, D)]):
            for chunk in range(nch):
                for half in range(2):
                    hw2 = chw // 2
                    dst_ap = wsb[:, chunk * chw + half * hw2:
                                 chunk * chw + (half + 1) * hw2]
                    src_ap = wt[chunk * 128:(chunk + 1) * 128,
                                half * hw2:(half + 1) * hw2]
                    if gidx % 2 == 0:
                        buf = wload[0][:, 0:hw2]
                        lws = lw[0]
                        cw = cvt_done.get(gidx - 2)
                        p.dma("sync", buf, src_ap, waits=[cw] if cw else [],
                              inc=lws)
                        p.act(dst_ap, buf, AF.Copy, scale=1.0 / 127.0,
                              waits=[(lws, lws.n)], inc=sc)
                        cvt_done[gidx] = (sc, sc.n)
                    else:
                        cs = [self.lwC, self.lwD][(gidx // 2) % 2]
                        if pend_scale.get(id(cs)):
                            da, ds = pend_scale[id(cs)].pop(0)
                            p.ts(da, da, 1.0 / 127.0, None, ALU.mult,
                                 waits=[ds], inc=ve)
                            scale_ve[id(cs)] = ve.n
                        wcd = [(ve, scale_ve[id(cs)])] if scale_ve.get(id(cs)) \
                            else []
                        p.dma("gpsimd", dst_ap, src_ap, waits=wcd, inc=cs)
                        pend_scale.setdefault(id(cs), []).append(
                            (dst_ap, (cs, cs.n)))
                    gidx += 1
            for key in list(pend_scale):
                for da, ds in pend_scale[key]:
                    p.ts(da, da, 1.0 / 127.0, None, ALU.mult, waits=[ds],
                         inc=ve)
                    scale_ve[key] = ve.n
                pend_scale[key] = []
            w_ready[mat] = [(sc, sc.n), (ve, ve.n)]
        w_ready_sc = sc.n
        w_ready_ve = ve.n
        esw.close()
        xn1 = sb("p2_xn1", [128, T], BF16)
        hcol = sb("p2_h", [128, 24 * W2], BF16)

        ab = self.pW1[:, 512:1024]
        bb = self.pW1[:, 1024:1536]
        gb = [self.pW0[:, 0:512], self.pW0[:, 512:1024]]
        ub = [self.pW0[:, 1024:1536], self.pW1[:, 0:512]]
        Y = self.p6[:, 0:512]

        src_r = src.rearrange("(c p) t -> p c t", p=128)
        dst_w = dst.rearrange("(c p) t -> p c t", p=128)

        sqbf = [self.sq[i].bitcast(BF16) for i in range(2)]
        xbv = [sqbf[i][:, 0:512] for i in range(2)]
        sbv = [sqbf[i][:, 512:1024] for i in range(2)]
        apply_of = {}
        store_done = {}
        sil_done = {}
        mul_done = {}
        evac_done = {}
        down_all_pe = 0

        def stage(tcn):
            lds = self.lxA
            prev_apply = apply_of.get(tcn - 1, 0)
            if tcn == 0:
                nld = tc0_ld
            else:
                p.dma("sync", x1col.ap().rearrange("p (c j) -> p c j", c=NDC),
                      src_r[:, :, tcn * W2:(tcn + 1) * W2],
                      waits=[(ve, prev_apply)], inc=lds)
                nld = lds.n
            bc = self.emit_stats(
                lambda i, b=x1col: b[:, i * W2:(i + 1) * W2], W2, NDC,
                ab, bb, [(lds, nld)],
                war_bank=[(ve, prev_apply)] if prev_apply else [],
                cast_bufs=(xbv, sbv))
            apply_of[tcn] = self.emit_apply(
                lambda dc, b=x1col: b[:, dc * W2:(dc + 1) * W2],
                lambda dc: xn1[:, dc * W2:(dc + 1) * W2],
                W2, NDC, ab, bb, bc)

        stage(0)
        for tc in range(NT2):
            apply_done = apply_of[tc]
            for ht in range(24):
                gbk = gb[ht % 2]
                ubk = ub[ht % 2]
                pht = ht - 2 if ht >= 2 else (22 + ht if tc > 0 else None)
                war_g = [(sc, sil_done[pht])] if pht in sil_done else []
                war_u = [(ve, mul_done[pht])] if pht in mul_done else []
                for dc in range(NDC):
                    p.mm(gbk, wg[:, dc * HID + ht * 128: dc * HID + (ht + 1) * 128],
                         xn1[:, dc * W2:(dc + 1) * W2], dc == 0, dc == NDC - 1,
                         waits=[(ve, apply_done)] + w_ready[0] +
                         (war_g if dc == 0 else []),
                         inc=pe if dc == NDC - 1 else None)
                g_done = pe.n
                for dc in range(NDC):
                    p.mm(ubk, wu[:, dc * HID + ht * 128: dc * HID + (ht + 1) * 128],
                         xn1[:, dc * W2:(dc + 1) * W2], dc == 0, dc == NDC - 1,
                         waits=w_ready[1] + (war_u if dc == 0 else []),
                         inc=pe if dc == NDC - 1 else None)
                u_done = pe.n
                hsl = hcol[:, ht * W2:(ht + 1) * W2]
                hw0 = [(pe, g_done)]
                if down_all_pe:
                    hw0.append((pe, down_all_pe))
                p.act(hsl, gbk, AF.Silu, waits=hw0, inc=sc)
                sil_done[ht] = sc.n
                p.tt(hsl, hsl, ubk, ALU.mult,
                     waits=[(sc, sil_done[ht]), (pe, u_done)], inc=ve)
                mul_done[ht] = ve.n
            h_done = ve.n
            if tc + 1 < NT2:
                stage(tc + 1)
            for dt in range(NDC):
                lr = [self.lxC, self.lxD][dt % 2]
                pdt2 = dt - 2 if dt - 2 >= 0 else dt + 6
                wl = [store_done[pdt2]] if pdt2 in store_done else []
                p.dma("sync", xres[dt % 2][:],
                      src[dt * 128:(dt + 1) * 128, tc * W2:(tc + 1) * W2],
                      waits=wl, inc=lr)
                lr_n = lr.n
                pdt = dt - 1 if dt >= 1 else (7 if tc > 0 else None)
                war_y = [(ve, evac_done[pdt])] if pdt in evac_done else []
                for hc in range(24):
                    p.mm(Y, wd[:, hc * D + dt * 128: hc * D + (dt + 1) * 128],
                         hcol[:, hc * W2:(hc + 1) * W2], hc == 0, hc == 23,
                         waits=[(ve, h_done)] + w_ready[2] +
                         (war_y if hc == 0 else []),
                         inc=pe if hc == 23 else None)
                y_done = pe.n
                gds = [gd, self.gdB][dt % 2]
                wst = [(pe, y_done), (lr, lr_n)]
                p.tt(xres[dt % 2][:], xres[dt % 2][:], Y, ALU.add,
                     waits=wst, inc=ve)
                evac_done[dt] = ve.n
                p.dma("gpsimd",
                      dst[dt * 128:(dt + 1) * 128, tc * W2:(tc + 1) * W2],
                      xres[dt % 2][:], waits=[(ve, ve.n)], inc=gds)
                store_done[dt] = (gds, gds.n)
            down_all_pe = pe.n
        es2.close()
        return gd.n

    def emit_p3(self):
        nc, p, d = self.nc, self.plan, self.d
        gd, sc, ve, pe = self.gd, self.sc, self.ve, self.pe
        lw = [self.lwA, self.lwB]
        lx = self.lxA
        src, dst = self.act_io(3)
        if 4 in self.phases:
            self._p4u_alloc()
        es3 = ExitStack()
        sb = lambda name, shape, dt: es3.enter_context(nc.sbuf_tensor(name, shape, dt))
        wq = sb("p3_wq", [128, NDC * 3 * D], BF16)
        wq_ch = {}
        for chunk in range(NDC):
            w = [(self.lwC, self.lwC.n)] if chunk > 0 else []
            p.dma("gpsimd", wq[:, chunk * 3 * D:(chunk + 1) * 3 * D],
                  d["wqkv"][chunk * 128:(chunk + 1) * 128, :], waits=w,
                  inc=self.lwC)
            wq_ch[chunk] = self.lwC.n
        wq_lwA = self.lwC.n

        x2col2 = [sb(f"p3_x2col{i}", [128, T], F32) for i in range(2)]
        xn2 = sb("p3_xn2", [128, T], BF16)
        qbuf = sb("p3_q", [128, NDC * 512], BF16)
        kbuf = sb("p3_k", [128, NDC * 1024], BF16)
        vbuf = sb("p3_v", [128, NDC * 1024], BF16)
        vtm = sb("p3_vtm", [128, 8 * NDC * 128], BF16)
        sT = sb("p3_sT", [128, 384], BF16)
        osb4 = sb("p3_osb4", [128, 4096], F32)
        abS = sb("p3_abS", [128, 512], F32)
        bbS = sb("p3_bbS", [128, 512], F32)
        xb16 = [sb(f"p3_xb16{i}", [128, 512], BF16) for i in range(2)]
        sqb16 = [sb(f"p3_sqb16{i}", [128, 512], BF16) for i in range(2)]

        qk_bank = [self.pW0[:, 0:512], self.pW0[:, 512:1024]]
        S3 = self.pW0[:, 1024:1408]
        Oa = self.pW1[:, 0:512]
        Ob = self.pW1[:, 512:1024]
        ab_t = self.p6
        bb_t = self.pW1[:, 1024:1536]
        Tb = self.p7[:, 0:128]
        Tbb = self.p7.bitcast(BF16)[:, 0:128]

        src_r = src.rearrange("(c p) t -> p c t", p=128)
        dst_w = dst.rearrange("(c p) t -> p c t", p=128)

        store_done = {}
        post_apply_ve = self.p6_setup_war
        pending_post = None
        evac_qk = {}
        s3_war_ve = 0
        s3_war_sc = 0
        self._p7war_ve = 0
        o_war = 0
        for tc in range(8):
            x2col = x2col2[tc % 2]
            lds = [self.lxA, self.lxB][tc % 2]
            w = [store_done[tc - 2]] if tc - 2 in store_done else []
            p.dma("sync", x2col.ap().rearrange("p (c j) -> p c j", c=NDC),
                  src_r[:, :, tc * 512:(tc + 1) * 512], waits=w, inc=lds)
            nld = lds.n
            bc = self.emit_stats(
                lambda i, b=x2col: b[:, i * 512:(i + 1) * 512], 512, NDC,
                ab_t[:], bb_t, [(lds, nld)],
                war_bank=[(ve, post_apply_ve)], cast_bufs=(xb16, sqb16))
            apply_ve = self.emit_apply(
                lambda dc, b=x2col: b[:, dc * 512:(dc + 1) * 512],
                lambda dc: xn2[:, dc * 512:(dc + 1) * 512],
                512, NDC, ab_t[:], bb_t, bc)
            if pending_post is not None:
                post_apply_ve = pending_post()
                pending_post = None
            s0 = (4 * tc) % 8
            for ot in range(24):
                bank = qk_bank[ot % 2]
                pot = ot - 2 if ot >= 2 else (22 + ot if tc > 0 else None)
                war = [(sc, evac_qk[pot])] if pot in evac_qk else []
                for dc in range(NDC):
                    p.mm(bank,
                         wq[:, dc * 3 * D + ot * 128: dc * 3 * D + (ot + 1) * 128],
                         xn2[:, dc * 512:(dc + 1) * 512], dc == 0, dc == NDC - 1,
                         waits=[(ve, apply_ve), (self.lwC, wq_ch[dc])] +
                         (war if dc == 0 else []),
                         inc=pe if dc == NDC - 1 else None)
                mmd = pe.n
                ew = [(pe, mmd)]
                if ot < 8:
                    out_ap = qbuf[:, ot * 512:(ot + 1) * 512]
                    p.act(out_ap, bank, AF.Identity, bias=self.bq_s[:, ot:ot + 1],
                          scale=1.0 / 512.0, waits=ew, inc=sc)
                elif ot < 16:
                    kk = ot - 8
                    out_ap = kbuf[:, kk * 1024 + s0 * 128: kk * 1024 + s0 * 128 + 512]
                    p.act(out_ap, bank, AF.Identity,
                          bias=self.bq_all[:, ot:ot + 1], waits=ew, inc=sc)
                else:
                    vv = ot - 16
                    out_ap = vbuf[:, vv * 1024 + s0 * 128: vv * 1024 + s0 * 128 + 512]
                    p.act(out_ap, bank, AF.Identity,
                          bias=self.bq_all[:, ot:ot + 1], waits=ew, inc=sc)
                evac_qk[ot] = sc.n
            qkv_sc = sc.n
            for i in range(4):
                kt = 4 * tc + i
                slot = kt % 8
                for dc in range(NDC):
                    p.trans(Tbb,
                            vbuf[:, dc * 1024 + slot * 128: dc * 1024 + (slot + 1) * 128],
                            self.ident_bf[:],
                            waits=[(sc, qkv_sc), (ve, self._p7war_ve)], inc=pe)
                    p.vcopy(vtm[:, (slot * NDC + dc) * 128:(slot * NDC + dc + 1) * 128],
                            Tbb, waits=[(pe, pe.n)], inc=ve, chain=False)
                    self._p7war_ve = ve.n
            vtm_ve = ve.n
            for qi in range(4):
                qt = 4 * tc + qi
                kts = [k for k in (qt - 2, qt - 1, qt) if k >= 0]
                for i, kt in enumerate(kts):
                    slot = kt % 8
                    w0 = [(sc, qkv_sc)]
                    if i == 0:
                        if s3_war_ve:
                            w0.append((ve, s3_war_ve))
                        if s3_war_sc:
                            w0.append((sc, s3_war_sc))
                    for dc in range(NDC):
                        p.mm(S3[:, i * 128:(i + 1) * 128],
                             kbuf[:, dc * 1024 + slot * 128: dc * 1024 + (slot + 1) * 128],
                             qbuf[:, dc * 512 + qi * 128: dc * 512 + (qi + 1) * 128],
                             dc == 0, dc == NDC - 1,
                             waits=(w0 if dc == 0 else []),
                             inc=pe if dc == NDC - 1 else None)
                smm = pe.n
                ev = []
                for i, kt in enumerate(kts):
                    sl = sT[:, i * 128:(i + 1) * 128]
                    if kt == qt:
                        p.tt(sl, S3[:, i * 128:(i + 1) * 128], self.m128ge[:],
                             ALU.mult, waits=[(pe, smm)], inc=ve)
                        ev.append((ve, ve.n))
                        s3_war_ve = ve.n
                    elif kt == qt - 2:
                        p.tt(sl, S3[:, i * 128:(i + 1) * 128], self.m128lt[:],
                             ALU.mult, waits=[(pe, smm)], inc=ve)
                        ev.append((ve, ve.n))
                        s3_war_ve = ve.n
                    else:
                        p.vcopy(sl, S3[:, i * 128:(i + 1) * 128],
                                waits=[(pe, smm)], inc=ve)
                        ev.append((ve, ve.n))
                        s3_war_ve = ve.n
                for dc in range(NDC):
                    obank = Oa if dc < 4 else Ob
                    col = (dc % 4) * 128
                    for i, kt in enumerate(kts):
                        slot = kt % 8
                        w2 = list(ev) + [(ve, vtm_ve)]
                        if i == 0 and dc == 0 and o_war:
                            w2.append((ve, o_war))
                        p.mm(obank[:, col:col + 128],
                             vtm[:, (slot * NDC + dc) * 128:(slot * NDC + dc + 1) * 128],
                             sT[:, i * 128:(i + 1) * 128],
                             i == 0, i == len(kts) - 1,
                             waits=w2,
                             inc=pe if i == len(kts) - 1 else None)
                av_done = pe.n
                for dc in range(NDC):
                    obank = Oa if dc < 4 else Ob
                    col = (dc % 4) * 128
                    p.vcopy(osb4[:, dc * 512 + qi * 128: dc * 512 + (qi + 1) * 128],
                            obank[:, col:col + 128],
                            waits=[(pe, av_done)] if dc == 0 else (), inc=ve)
                o_war = ve.n
            osb_ve = ve.n
            ab2 = self.pW0[:, 0:512]
            bb2 = self.pW0[:, 512:1024]
            bc2 = self.emit_stats(
                lambda i, b=osb4: b[:, i * 512:(i + 1) * 512], 512, NDC,
                ab2, bb2, [(ve, osb_ve)],
                war_bank=[(ve, post_apply_ve), (sc, qkv_sc)],
                cast_bufs=(xb16, sqb16))
            p.vcopy(abS[:], ab2, waits=[(pe, bc2), (ve, post_apply_ve)], inc=ve)
            p.vcopy(bbS[:], bb2, inc=ve)
            ab_copied = ve.n

            def flush_post(x2c=x2col, tcc=tc, abc=ab_copied):
                for dc in range(NDC):
                    tm = self.sq[dc % 2][:, 0:512]
                    w2 = [(ve, abc)]
                    if dc < 2 and self.sq_war:
                        w2.append((pe, self.sq_war))
                    p.tt(tm, osb4[:, dc * 512:(dc + 1) * 512], abS[:], ALU.mult,
                         waits=w2, inc=ve)
                    p.tt(tm, tm, bbS[:], ALU.add, inc=ve)
                    p.ts(tm, tm, self.lnw_sb[:, dc:dc + 1],
                         self.lnb_sb[:, dc:dc + 1], ALU.mult, ALU.add, inc=ve)
                    xsl = x2c[:, dc * 512:(dc + 1) * 512]
                    p.tt(xsl, xsl, tm, ALU.add, inc=ve)
                self.sq_ve_war = ve.n
                gds = [gd, self.gdB][tcc % 2]
                p.dma("gpsimd", dst_w[:, :, tcc * 512:(tcc + 1) * 512],
                      x2c.ap().rearrange("p (c j) -> p c j", c=NDC),
                      waits=[(ve, ve.n)], inc=gds)
                store_done[tcc] = (gds, gds.n)
                return ve.n

            pending_post = flush_post
            if 4 in self.phases:
                self._p4u_step()
        if pending_post is not None:
            post_apply_ve = pending_post()
        es3.close()
        return max(gd.n, 1)

    def _p4u_alloc(self):
        nc = self.nc
        self.p4u_es = ExitStack()
        self.p4u_wt = self.p4u_es.enter_context(
            nc.sbuf_tensor("p4_wt", [128, NDC * D], BF16))
        self.p4u_stg = ExitStack()
        sbs = lambda n, s, t: self.p4u_stg.enter_context(nc.sbuf_tensor(n, s, t))
        self.p4u_pkt = [sbs(f"p4_pk{i}", [128, 256], I32) for i in range(2)]
        self.p4u_ei = sbs("p4_ei", [128, 256], I32)
        self.p4u_wrow = [sbs(f"p4_wr{i}", [128, D], F32) for i in range(2)]
        self.p4u_wrow_done = {}
        self.p4u_ot = 0
        self.p4u_pe_war = 0
        self._p7war_ve = getattr(self, "_p7war_ve", 0)

    def _p4u_step(self):
        if self.p4u_ot >= NDC:
            return
        ot = self.p4u_ot
        self.p4u_ot += 1
        p, d = self.plan, self.d
        ve, pe = self.ve, self.pe
        buf = self.p4u_pkt[ot % 2]
        lws = [self.lwA, self.lwB][ot % 2]
        w = [(ve, self.p4u_wrow_done[ot - 2])] \
            if ot - 2 in self.p4u_wrow_done else []
        p.dma("sync", buf[:], d["pkd"][ot * 128:(ot + 1) * 128, :],
              waits=w, inc=lws)
        nld = lws.n
        wr = self.p4u_wrow[ot % 2]
        ei = self.p4u_ei
        wrw = [(pe, self.p4u_pe_war)] if self.p4u_pe_war else []
        for j in range(4):
            if j == 0:
                p.ts(ei[:], buf[:], 3, None, ALU.bitwise_and,
                     waits=[(lws, nld)] + wrw, inc=ve)
            else:
                p.ts(ei[:], buf[:], 2 * j, None, ALU.logical_shift_right,
                     inc=ve)
                p.ts(ei[:], ei[:], 3, None, ALU.bitwise_and, inc=ve)
            p.ts(wr[:, j:D:4], ei[:], -1.0, None, ALU.add, inc=ve)
        wrd = ve.n
        Tb = self.p7[:, 0:128]
        for ic in range(NDC):
            p.trans(Tb, wr[:, ic * 128:(ic + 1) * 128], self.ident_f32[:],
                    waits=[(ve, wrd), (ve, self._p7war_ve)], inc=pe)
            self.p4u_pe_war = pe.n
            p.vcopy(self.p4u_wt[:, ic * D + ot * 128: ic * D + (ot + 1) * 128],
                    Tb, waits=[(pe, pe.n)], inc=ve, chain=False)
            self._p7war_ve = ve.n
        self.p4u_wrow_done[ot] = ve.n
        self.p4u_wt_ready = ve.n

    def emit_p4(self):
        nc, p, d = self.nc, self.plan, self.d
        gd, sc, ve, pe = self.gd, self.sc, self.ve, self.pe
        lw = [self.lwA, self.lwB]
        lx = self.lxA
        src, dst = self.act_io(4)
        es4 = ExitStack()
        sb = lambda name, shape, dt: es4.enter_context(nc.sbuf_tensor(name, shape, dt))
        if not hasattr(self, "p4u_ot"):
            self._p4u_alloc()
        while self.p4u_ot < NDC:
            self._p4u_step()
        wt = self.p4u_wt
        wt_ready = self.p4u_wt_ready
        wt_pe = self.p4u_pe_war
        self.p4u_stg.close()

        x3col2 = [sb(f"p4_x3col{i}", [128, T], F32) for i in range(2)]
        xn3 = sb("p4_xn3", [128, T], F32)
        xtm = sb("p4_xtm", [128, T], F32)
        qfm = sb("p4_qfm", [128, T], BF16)
        xqb = [sb(f"p4_xqb{i}", [128, 1024], BF16) for i in range(2)]
        xb16 = [sb(f"p4_xb16{i}", [128, 512], BF16) for i in range(2)]
        qint = sb("p4_qint", [128, 1024], I32)
        sqb16 = [sb(f"p4_sqb16{i}", [128, 512], BF16) for i in range(2)]
        mcolp = [sb(f"p4_mcol{i}", [128, 1], F32) for i in range(2)]
        mtmpp = [sb(f"p4_mcol2{i}", [128, 1], F32) for i in range(2)]
        ascolp = [sb(f"p4_ascol{i}", [128, 1], F32) for i in range(2)]
        invcolp = [sb(f"p4_invcol{i}", [128, 1], F32) for i in range(2)]

        ab = self.pW0[:, 0:512]
        bb = self.pW0[:, 512:1024]
        Tf = [self.pW0[:, 1024:1152], self.pW1[:, 1024:1152]]
        Tfb = [self.pW0.bitcast(BF16)[:, 2048:2176],
               self.pW1.bitcast(BF16)[:, 2048:2176]]
        yb = [self.p6[:, 0:512], self.pW1[:, 0:512]]

        src_r = src.rearrange("(c p) t -> p c t", p=128)
        dst_w = dst.rearrange("(c p) t -> p c t", p=128)

        store_done = {}
        apply_done = self.p6_setup_war
        tf_war = [0, 0]
        epi_done = {}
        for tc in range(8):
            x3col = x3col2[tc % 2]
            lds = [self.lxA, self.lxB][tc % 2]
            w = [store_done[tc - 2]] if tc - 2 in store_done else []
            if tc == 0:
                w += [(ve, wt_ready), (pe, wt_pe)]
            p.dma("sync", x3col.ap().rearrange("p (c j) -> p c j", c=NDC),
                  src_r[:, :, tc * 512:(tc + 1) * 512], waits=w, inc=lds)
            nld = lds.n
            bc = self.emit_stats(
                lambda i, b=x3col: b[:, i * 512:(i + 1) * 512], 512, NDC,
                ab, bb, [(lds, nld)],
                war_bank=[(ve, apply_done)], cast_bufs=(xb16, sqb16))
            apply_done = self.emit_apply(
                lambda dc, b=x3col: b[:, dc * 512:(dc + 1) * 512],
                lambda dc: xn3[:, dc * 512:(dc + 1) * 512],
                512, NDC, ab, bb, bc)
            adc = list(self.apply_dc)
            for ti in range(4):
                for dc in range(NDC):
                    p.trans(Tf[dc % 2],
                            xn3[:, dc * 512 + ti * 128: dc * 512 + (ti + 1) * 128],
                            self.ident_f32[:],
                            waits=[(ve, adc[dc]), (ve, tf_war[dc % 2])], inc=pe)
                    p.vcopy(xtm[:, ti * 1024 + dc * 128: ti * 1024 + (dc + 1) * 128],
                            Tf[dc % 2], waits=[(pe, pe.n)], inc=ve)
                    tf_war[dc % 2] = ve.n
            qdone = {}
            xqb_pe = getattr(self, "_p4_xqb_pe", [0, 0])

            def bwd(ti):
                xq = xqb[ti % 2]
                for dc in range(NDC):
                    p.trans(Tfb[dc % 2], xq[:, dc * 128:(dc + 1) * 128],
                            self.ident_bf[:],
                            waits=[(ve, qdone[ti]), (ve, tf_war[dc % 2])],
                            inc=pe)
                    xqb_pe[ti % 2] = pe.n
                    p.vcopy(qfm[:, dc * 512 + ti * 128: dc * 512 + (ti + 1) * 128],
                            Tfb[dc % 2], waits=[(pe, pe.n)], inc=ve,
                            chain=False)
                    tf_war[dc % 2] = ve.n

            asdone = {}

            def qA(ti):
                tmi = xtm[:, ti * 1024:(ti + 1) * 1024]
                mc, iv = mcolp[ti % 2], invcolp[ti % 2]
                p.vmax(mc[:], tmi, True, inc=ve)
                p.ts(mc[:], mc[:], 1e-5, None, ALU.max, inc=ve)
                p.ts(iv[:], mc[:], 1.0 / 127.0, None, ALU.mult, inc=ve)
                inv_ve = ve.n
                p.act(mtmpp[ti % 2][:], iv[:], AF.Ln, waits=[(ve, inv_ve)],
                      inc=sc)
                p.act(ascolp[ti % 2][:], mtmpp[ti % 2][:], AF.Exp, scale=-1.0,
                      inc=sc)
                asdone[ti] = sc.n

            def qC(ti):
                tmi = xtm[:, ti * 1024:(ti + 1) * 1024]
                p.ts(qint[:], tmi, ascolp[ti % 2][:], -128.0, ALU.mult,
                     ALU.max, waits=[(sc, asdone[ti])], inc=ve)
                p.ts(xqb[ti % 2][:], qint[:], 127.0, invcolp[ti % 2][:],
                     ALU.min, ALU.mult,
                     waits=[(pe, xqb_pe[ti % 2])] if xqb_pe[ti % 2] else [],
                     inc=ve)
                qdone[ti] = ve.n

            for ti in range(4):
                qA(ti)
                if ti >= 1:
                    qC(ti - 1)
                if ti >= 2:
                    bwd(ti - 2)
            qC(3)
            bwd(2)
            bwd(3)
            self._p4_xqb_pe = xqb_pe
            q_ready = ve.n
            for ot in range(NDC):
                ybk = yb[ot % 2]
                pot = ot - 2 if ot >= 2 else (6 + ot if tc > 0 else None)
                war = [(sc, epi_done[pot])] if pot in epi_done else []
                if ot == 0 and tc == 0:
                    war.append((ve, self.p6_setup_war))
                for ic in range(NDC):
                    p.mm(ybk, wt[:, ic * D + ot * 128: ic * D + (ot + 1) * 128],
                         qfm[:, ic * 512:(ic + 1) * 512], ic == 0, ic == NDC - 1,
                         waits=[(ve, q_ready), (ve, wt_ready)] +
                         (war if ic == 0 else []),
                         inc=pe if ic == NDC - 1 else None)
                y_done = pe.n
                tm = self.sq[ot % 2][:, 0:512]
                wts = [(pe, y_done)] + war
                p.stt(tm, ybk, self.ws_col[:], x3col[:, ot * 512:(ot + 1) * 512],
                      ALU.mult, ALU.add, waits=wts, inc=ve)
                p.act(x3col[:, ot * 512:(ot + 1) * 512], tm, AF.Identity,
                      bias=self.bias1_sb[:, ot:ot + 1], waits=[(ve, ve.n)], inc=sc)
                epi_done[ot] = sc.n
            self.sq_ve_war = ve.n
            self.sq_war = 0
            last_sc = sc.n
            gds = [gd, self.gdB][tc % 2]
            p.dma("gpsimd", dst_w[:, :, tc * 512:(tc + 1) * 512],
                  x3col.ap().rearrange("p (c j) -> p c j", c=NDC),
                  waits=[(sc, last_sc)], inc=gds)
            store_done[tc] = (gds, gds.n)
        es4.close()
        self.p4u_es.close()
        return gd.n

    def build(self):
        self.declare_dram()
        self.alloc_persistent()
        self.emit_setup()
        emitters = {1: self.emit_p1, 2: self.emit_p2, 3: self.emit_p3,
                    4: self.emit_p4}
        for ph in self.phases:
            final_gd = emitters[ph]()
            self.plan.barrier([(self.gd, self.gd.n), (self.gdB, self.gdB.n),
                               (self.ve, self.ve.n),
                               (self.pe, self.pe.n), (self.sc, self.sc.n),
                               (self.lxA, self.lxA.n), (self.lxB, self.lxB.n),
                               (self.lwA, self.lwB.n and self.lwA.n or self.lwA.n),
                               (self.lwB, self.lwB.n)])
        self.plan.wait_only("gpsimd", [(self.gd, self.gd.n),
                                       (self.gdB, self.gdB.n)])
        block = self.es.enter_context(self.nc.Block())
        self.plan.realize(block)
        self.es.close()
        return self.nc


_NC_CACHE = {}


def _get_nc(phases):
    key = tuple(phases)
    if key not in _NC_CACHE:
        _NC_CACHE[key] = Builder(key).build()
    return _NC_CACHE[key]


def _stage_consts(inputs):
    f32 = np.float32
    g = lambda k: np.asarray(inputs[k])
    c = {}
    c["bvq"] = np.ascontiguousarray(
        g("L0_attn_bv_q").astype(f32).reshape(-1).reshape(8, 128).T)
    c["bvk"] = np.ascontiguousarray(
        g("L0_attn_bv_k").astype(f32).reshape(-1).reshape(8, 128).T)
    c["bvv"] = np.ascontiguousarray(
        g("L0_attn_bv_v").astype(f32).reshape(-1).reshape(8, 128).T)
    c["w0g"] = np.ascontiguousarray(g("L0_ffn_w_gate").astype(np.int32).T)
    c["w0u"] = np.ascontiguousarray(g("L0_ffn_w_up").astype(np.int32).T)
    c["w0d"] = np.ascontiguousarray(g("L0_ffn_w_down").astype(np.int32).T)
    c["wqkv"] = np.ascontiguousarray(g("L1_attn_W_qkv").astype(f32))
    c["bqkv"] = np.ascontiguousarray(
        g("L1_attn_b_qkv").astype(f32).reshape(24, 128).T)
    c["lnw"] = np.ascontiguousarray(
        g("L1_attn_ln_w").astype(f32).reshape(8, 128).T)
    c["lnb"] = np.ascontiguousarray(
        g("L1_attn_ln_b").astype(f32).reshape(8, 128).T)
    c["pkd"] = np.ascontiguousarray(
        g("L1_ffn_packed_weight").astype(np.int32).reshape(1024, 256))
    c["wsc"] = np.ascontiguousarray(
        g("L1_ffn_weight_scale").astype(f32).reshape(1, 1))
    c["bias1"] = np.ascontiguousarray(
        g("L1_ffn_bias").astype(f32).reshape(8, 128).T)
    return c


def kernel(**inputs):
    x = np.asarray(inputs["L0_attn_x"]).astype(np.float32)
    consts = _stage_consts(inputs)
    cur = [np.ascontiguousarray(x[b].T) for b in range(8)]
    for phases in PHASE_GROUPS:
        nc = _get_nc(phases)
        in_maps = [dict(consts, XIN=cur[b]) for b in range(8)]
        res = bass_utils.run_bass_kernel_spmd(nc, in_maps, core_ids=list(range(8)))
        cur = [np.asarray(res.results[b]["XOUT"]) for b in range(8)]
    out = np.stack([cur[b].T for b in range(8)], axis=0)
    return out.astype(np.float32)
